# version 02
# baseline (speedup 1.0000x reference)
import contextlib
import numpy as np
import ml_dtypes
import concourse.bass as bass
import concourse.mybir as mybir
from concourse.bass_utils import run_bass_kernel_spmd

F32 = mybir.dt.float32
BF16 = mybir.dt.bfloat16
AF = mybir.ActivationFunctionType
ALU = mybir.AluOpType
AX = mybir.AxisListType.X
BF = ml_dtypes.bfloat16

MAGIC = float(np.float32(12582912.0))
H, HD, D, DFF, C = 16, 64, 1024, 4096, 64
EPS = 1e-5

AMAX, NEGAMIN, SX, INVSX, ZX, ZXM, SSW1, NEGZSS1, SW1, INVSW1 = range(10)
SW2, INVSW2, HMAX, HNEGMIN, SH, INVSH, ZH, ZHM, SSW2, NEGZSS2 = range(10, 20)


NLANE = 8


class Em:

    def __init__(self, mode, sems, cp, cptr=None):
        self.mode = mode
        self.sems = sems
        self.cnt = {k: 0 for k in ("tensor", "vector", "scalar", "gpsimd",
                                   "sync", "dY", "dG", "cc")}
        self.cp = cp
        self.cptr = cptr if cptr is not None else {}
        self.h = None
        self.val2tag = None

    def at(self, tag):
        if tag not in self.cp:
            raise KeyError(f"checkpoint {tag!r} used before recorded")
        return self.cp[tag]


    def op(self, track, fn, waits=(), inc=0, dma=False, cc=False, tag=None,
           flush=False, tag_all=False):
        lane, lv = None, 0
        if dma:
            qs = "dY" if track == "sync" else "dG"
            idx = self.cnt[qs]
            self.cnt[qs] = idx + 1
            lane = f"{qs}{idx % NLANE}"
            lv = self.cnt.get(lane, 0) + 16
            self.cnt[lane] = lv
        if self.mode == "plan":
            for wtag in waits:
                if wtag not in self.cp:
                    raise KeyError(f"wait on unrecorded checkpoint {wtag!r}")
            if cc:
                self.cnt["cc"] += 1
            if inc:
                self.cnt[track] += inc
            if tag:
                if tag_all:
                    snap = []
                    for q2 in ("dY", "dG"):
                        for i in range(NLANE):
                            v = self.cnt.get(f"{q2}{i}", 0)
                            if v:
                                snap.append((f"{q2}{i}", v))
                    self.cp[tag] = tuple(snap)
                    self.cptr[tag] = "MULTI"
                else:
                    self.cp[tag] = (lv if dma else
                                    self.cnt["cc"] if cc else self.cnt[track])
                    self.cptr[tag] = (lane if dma else "cc" if cc
                                      else track)
            return
        if self.mode != track:
            return
        for wtag in waits:
            if self.cptr[wtag] == "MULTI":
                for ln2, v2 in self.cp[wtag]:
                    self.h.wait_ge(self.sems[ln2], v2)
            else:
                self.h.wait_ge(self.sems[self.cptr[wtag]], self.cp[wtag])
        if dma and lv > 16:
            self.h.wait_ge(self.sems[lane], lv - 16)
        if flush:
            for q2 in ("dY", "dG"):
                for i in range(NLANE):
                    v = self.cnt.get(f"{q2}{i}", 0)
                    if v:
                        self.h.wait_ge(self.sems[f"{q2}{i}"], v)
        ins = fn(self.h)
        if dma:
            ins.then_inc(self.sems[lane], 16)
        if cc:
            ins.then_inc(self.sems["cc"], 1)
        if inc:
            ins.then_inc(self.sems[track], inc)


def make_consts():
    c = {}
    tri = np.tril(np.ones((C, C), np.float32))
    a = -(tri.T - 0.5)
    b = -(1.0 - tri.T)
    za = np.zeros((128, 128), np.float32)
    zb = np.zeros((128, 128), np.float32)
    za[0:64, 0:64] = a; za[64:128, 64:128] = a
    zb[0:64, 0:64] = b; zb[64:128, 64:128] = b
    c["trioA"], c["trioB"] = za, zb
    tl = np.zeros((128, 16 * 32), np.float32)
    for q in range(16):
        tl[0:64, q * 32 + 2 * q] = -1.0
        tl[64:128, q * 32 + 2 * q + 1] = -1.0
    c["triLc"] = tl
    nb = np.zeros((128, 128), BF)
    nb[0:64, 0:64] = -1.0
    nb[64:128, 64:128] = -1.0
    c["negones"] = nb
    sr = np.zeros((64, 64 * 64), BF)
    for ci in range(64):
        sr[ci, ci * 64:(ci + 1) * 64] = 1.0
    c["selrow"] = sr
    ta = np.zeros((64, 65), np.float32)
    for m in range(64):
        ta[:m, m] = 1.0
        ta[m, m] = 0.5
    ta[:, 64] = 1.0
    c["triA65"] = ta
    c["maskc"] = np.asarray(tri.T, BF)
    c["ident_bf"] = np.eye(128, dtype=BF)
    c["ident_f32"] = np.eye(128, dtype=np.float32)
    c["ones_f32"] = np.ones((128, 128), np.float32)
    c["ones_bf"] = np.ones((1, 128), BF)
    return c


def build_nc(T, min_on_gpsimd=False, phase_limit=99, dumps=(), ser=0,
             dbg2=False):
    NCH = T // C
    NPAIR = NCH // 2
    NST = T // 512
    NTI = T // 128
    NOC = DFF // 128

    nc = bass.Bass("TRN2", target_bir_lowering=False, detect_race_conditions=False)
    consts = make_consts()

    dx = nc.dram_tensor("x", [T, D], F32, kind="ExternalInput").ap()
    dW = {w: nc.dram_tensor(w, [D, D], F32, kind="ExternalInput").ap()
          for w in ("Wq", "Wk", "Wv", "Wf", "Wg", "Wout")}
    dbf = nc.dram_tensor("bfv", [1, D], F32, kind="ExternalInput").ap()
    dgam = nc.dram_tensor("gam", [1, D], F32, kind="ExternalInput").ap()
    dbet = nc.dram_tensor("bet", [1, D], F32, kind="ExternalInput").ap()
    dW1 = nc.dram_tensor("W1", [DFF, D], F32, kind="ExternalInput").ap()
    db1 = nc.dram_tensor("b1", [32, 128], F32, kind="ExternalInput").ap()
    dW2 = nc.dram_tensor("W2", [D, DFF], F32, kind="ExternalInput").ap()
    db2 = nc.dram_tensor("b2", [8, 128], F32, kind="ExternalInput").ap()
    dmm = nc.dram_tensor("m64", [128, 8], F32, kind="ExternalInput").ap()
    dmc = nc.dram_tensor("m64c", [128, 8], F32, kind="ExternalInput").ap()
    dout = nc.dram_tensor("out", [T, D], F32, kind="ExternalOutput").ap()

    d_k = nc.dram_tensor("d_k", [NST, 128, 4096], BF16).ap()
    d_vt = nc.dram_tensor("d_vt", [NPAIR, 128, 1024], BF16).ap()
    d_at = nc.dram_tensor("d_at", [NPAIR, 128, 1024], BF16).ap()
    d_sloc = nc.dram_tensor("d_sloc", [NCH + 1, 128, 512], BF16).ap()
    d_gn = nc.dram_tensor("d_gn", [NPAIR, 128, 1024], BF16).ap()
    d_x1 = nc.dram_tensor("d_x1", [NTI, 128, 1024], F32).ap()
    d_h = nc.dram_tensor("d_h", [NST, NOC, 128, 512], BF16).ap()
    if dbg2:
        d2_qb = nc.dram_tensor("d2_qb", [128, 4096], BF16,
                               kind="ExternalOutput").ap()
        d2_stm = nc.dram_tensor("d2_stm", [2, 128, 1024], BF16,
                                kind="ExternalOutput").ap()
        d2_ssb = nc.dram_tensor("d2_ssb", [2, 128, 512], BF16,
                                kind="ExternalOutput").ap()
        d2_om = nc.dram_tensor("d2_om", [128, 1024], F32,
                               kind="ExternalOutput").ap()
        d2_wa = nc.dram_tensor("d2_wa", [128, 8192], BF16,
                               kind="ExternalOutput").ap()
        d2_er = nc.dram_tensor("d2_er", [128, 2048], BF16,
                               kind="ExternalOutput").ap()
        d2_sp = nc.dram_tensor("d2_sp", [128, 2560], F32,
                               kind="ExternalOutput").ap()
        d2_lc = nc.dram_tensor("d2_lc", [64, 1024], F32,
                               kind="ExternalOutput").ap()
        d2_cols = nc.dram_tensor("d2_cols", [128, 4], F32,
                                 kind="ExternalOutput").ap()
        d2_om2 = nc.dram_tensor("d2_om2", [128, 1024], F32,
                                kind="ExternalOutput").ap()
        d2_xqf = nc.dram_tensor("d2_xqf", [128, 4096], BF16,
                                kind="ExternalOutput").ap()
        d2_w1t = nc.dram_tensor("d2_w1t", [128, 32768], BF16,
                                kind="ExternalOutput").ap()
        d2_cs1 = nc.dram_tensor("d2_cs1", [128, 32], F32,
                                kind="ExternalOutput").ap()
        d2_scb = nc.dram_tensor("d2_scb", [128, 16], F32,
                                kind="ExternalOutput").ap()
        d2_b1e = nc.dram_tensor("d2_b1e", [128, 32], F32,
                                kind="ExternalOutput").ap()
        d2_xf = nc.dram_tensor("d2_xf", [128, 4096], BF16,
                               kind="ExternalOutput").ap()
        d2_xf0 = nc.dram_tensor("d2_xf0", [128, 4096], BF16,
                                kind="ExternalOutput").ap()
        d2_xf1 = nc.dram_tensor("d2_xf1", [128, 4096], BF16,
                                kind="ExternalOutput").ap()
        d2_xf2 = nc.dram_tensor("d2_xf2", [128, 4096], BF16,
                                kind="ExternalOutput").ap()
    d_agin = nc.dram_tensor("d_agin", [256, 512], F32).ap()
    d_agout = nc.dram_tensor("d_agout", [8, 256, 512], F32).ap()
    d_mm1i = nc.dram_tensor("d_mm1i", [2], F32).ap()
    d_mm1o = nc.dram_tensor("d_mm1o", [2], F32).ap()
    d_mm2i = nc.dram_tensor("d_mm2i", [2], F32).ap()
    d_mm2o = nc.dram_tensor("d_mm2o", [2], F32).ap()
    d_consts = {k: nc.inline_tensor(v, name="c_" + k) for k, v in consts.items()}

    XCOLS = max(8 * T, 8 * DFF)
    G1C = 32768
    G2C = 13312
    FAC = 8192
    with contextlib.ExitStack() as es:
        t = {}
        B = {}

        def sb(name, shape, dtype):
            t[name] = es.enter_context(nc.sbuf_tensor(name, shape, dtype))

        sb("AX", [128, XCOLS], BF16)
        sb("AG1", [128, G1C], BF16)
        sb("AG2", [128, G2C], BF16)
        sb("AF", [128, FAC], F32)

        def alias(name, arena, base, cols):
            B[name] = (arena, base)
            t[name] = t[arena].ap()[:, base:base + cols]

        alias("xfm", "AX", 0, 8 * T)
        alias("W1qT", "AX", 0, 8 * DFF)
        alias("wA", "AG1", 0, 8192)
        alias("wB", "AG1", 8192, 8192)
        alias("wC", "AG1", 16384, 8192)
        alias("kbuf0", "AG1", 24576, 4096)
        alias("kbuf1", "AG1", 28672, 4096)
        alias("hq", "AG1", 0, 16384)
        alias("hstage0", "AG1", 16384, 4096)
        alias("hstage1", "AG1", 20480, 4096)
        alias("xtm0", "AG2", 0, 1024)
        alias("xtm1", "AG2", 1024, 1024)
        alias("sph", "AG2", 0, 1024)
        alias("spl", "AG2", 1024, 1024)
        alias("abuf0", "AG2", 2048, 1024)
        alias("abuf1", "AG2", 3072, 1024)
        alias("vtb0", "AG2", 4096, 1024)
        alias("vtb1", "AG2", 5120, 1024)
        alias("wtb", "AG2", 6144, 1024)
        alias("ktm", "AG2", 7168, 1024)
        alias("kfmr0", "AG2", 8192, 1024)
        alias("kfmr1", "AG2", 9216, 1024)
        alias("slocbf0", "AG2", 10240, 512)
        alias("slocbf1", "AG2", 10752, 512)
        alias("qbuf0", "AG2", 0, 4096)
        alias("vtr0", "AG2", 4096, 1024)
        alias("vtr1", "AG2", 5120, 1024)
        alias("atr0", "AG2", 6144, 1024)
        alias("atr1", "AG2", 7168, 1024)
        alias("gnr0", "AG2", 8192, 1024)
        alias("gnr1", "AG2", 9216, 1024)
        alias("stm", "AG2", 10240, 1024)
        alias("SSb", "AG2", 11264, 512)
        alias("slocr0", "AG2", 11776, 512)
        alias("slocr1", "AG2", 12288, 512)
        alias("gated", "AG2", 0, 1024)
        alias("gfm", "AG2", 1024, 1024)
        alias("xq", "AG2", 2048, 1024)
        alias("xqfm0", "AG2", 3072, 4096)
        alias("xqfm1", "AG2", 7168, 4096)
        alias("hbf", "AG2", 11264, 2048)
        alias("hjunk", "AG2", 2048, 512)
        alias("w2qbf", "AG2", 3072, 4096)
        alias("vbuf", "AF", 0, 1024)
        alias("ebuf", "AF", 1024, 1024)
        alias("spbuf", "AF", 2048, 1024)
        alias("eibuf", "AF", 3072, 1024)
        alias("ewbuf", "AF", 4096, 1024)
        alias("acu", "AF", 5120, 1024)
        alias("om", "AF", 0, 1024)
        alias("gnorm", "AF", 1024, 1024)
        alias("silb", "AF", 2048, 1024)
        alias("r1", "AF", 3072, 1024)
        alias("x1t", "AF", 4096, 1024)
        alias("sq", "AF", 5120, 1024)
        alias("gamma_bc", "AF", 6144, 1024)
        alias("beta_bc", "AF", 7168, 1024)
        alias("xt0", "AF", 0, 1024)
        alias("xt1", "AF", 1024, 1024)
        alias("x1r0", "AF", 0, 1024)
        alias("x1r1", "AF", 1024, 1024)
        alias("hf0", "AF", 2048, 512)
        alias("hf1", "AF", 2560, 512)
        alias("wtile0", "AF", 2048, 1024)
        alias("wtile1", "AF", 3072, 1024)
        alias("w2t0", "AF", 0, 4096)
        alias("w2t1", "AF", 4096, 4096)
        alias("sq4", "AF", 4096, 4096)
        alias("b2eff_bc", "AF", 3072, 1024)

        for k, v in consts.items():
            if k == "selrow":
                continue
            sb(k, list(v.shape),
               F32 if v.dtype == np.dtype(np.float32) else BF16)
        t["selrow"] = t["AG1"].ap()[0:64, 16384:20480]
        sb("bf_sb", [1, D], BF16)
        sb("m64s", [128, 8], F32)
        sb("m64cs", [128, 8], F32)
        sb("onecol", [128, 1], F32)
        sb("zerocol", [128, 1], F32)
        sb("negMcol", [128, 1], F32)
        sb("SP", [128, 2560], F32)
        alias2 = lambda name, base, cols: (B.__setitem__(name, ("SP", base)),
                                           t.__setitem__(name, t["SP"].ap()[:, base:base + cols]))
        alias2("Scur", 0, 512)
        alias2("tmpS", 512, 512)
        alias2("dbuf", 1024, 512)
        alias2("zeroS", 1536, 512)
        alias2("dex_st", 2048, 512)
        alias2("S_in", 1024, 512)
        alias2("Sg", 512, 512)
        alias2("Dg", 1536, 512)
        alias2("coef", 512, 512)
        alias2("cfold", 2048, 512)
        alias2("ssA", 1536, 512)
        sb("col0", [128, 1], F32)
        sb("col1", [128, 1], F32)
        sb("col2", [128, 1], F32)
        sb("col3", [128, 1], F32)
        sb("mmx", [128, 2], F32)
        sb("wabs", [128, 1], F32)
        sb("sc", [1, 32], F32)
        sb("scbc", [128, 16], F32)
        sb("b1T", [128, 32], F32)
        sb("b2T", [128, 8], F32)
        sb("b1raw", [32, 128], F32)
        sb("b2raw", [8, 128], F32)
        sb("colsum1", [128, 32], F32)
        sb("colsum2", [128, 8], F32)
        sb("b1eff", [128, 32], F32)
        sb("b2eff", [128, 8], F32)
        sb("maxc", [128, 32], F32)
        sb("nminc", [128, 32], F32)
        sb("mm2", [2, 128], F32)
        sb("mm2c", [2, 1], F32)
        sb("tpad", [128, 2], F32)
        t["b2eff_bc"] = t["AG1"].ap().bitcast(F32)[:, 12288:13312]
        B["b2eff_bc"] = None
        t["Lc_sb"] = t["AF"].ap()[0:64, 0:1024]
        t["e1r"] = t["AG1"].ap()[0:64, 8192:9216]
        t["e2r"] = t["AG1"].ap()[0:64, 9216:10240]
        t["gam_sb"] = t["AF"].ap()[0:1, 6144:7168]
        t["bet_sb"] = t["AF"].ap()[0:1, 7168:8192]
        t["b2eff_row"] = t["AG1"].ap().bitcast(F32)[0:1, 12288:13312]

        P4 = es.enter_context(nc.psum_tensor("P4", [128, 2048], F32))
        P_lc = es.enter_context(nc.psum_tensor("P_lc", [128, 1024], F32))
        P_s = es.enter_context(nc.psum_tensor("P_s", [128, 512], F32))
        P_b7 = es.enter_context(nc.psum_tensor("P_b7", [128, 512], F32))

        sems = {k: es.enter_context(nc.semaphore("s_" + k))
                for k in ("tensor", "vector", "scalar", "gpsimd", "sync",
                          "cc")
                + tuple(f"dY{i}" for i in range(NLANE))
                + tuple(f"dG{i}" for i in range(NLANE))}
        cp = {}
        cptr = {}

        def walk(em: Em):
            E, A = em.op, em.at

            def mm(e, out, lhsT, rhs, st, sp, tp=None, sg=False):
                return e.matmul(out, lhsT, rhs, start=st, stop=sp,
                                tile_position=tp, skip_group_check=sg)

            PKX = P_lc.ap().bitcast(BF16)[:, 0:1024]
            PKM = P4.ap().bitcast(BF16)[:, 2048:3072]
            PKG = P4.ap().bitcast(BF16)[:, 0:1024]

            def xfm_rhs(kk, st):
                return bass.AP(t["AX"].ap().tensor,
                               st * 4 * 1024 + kk * 128,
                               [[XCOLS, 128], [1024, 4], [1, 128]])

            def vpad():
                E("vector", lambda e: e.tensor_scalar(
                    out=t["onecol"][:], in0=t["onecol"][:], scalar1=1.0,
                    scalar2=None, op0=ALU.mult), inc=0)

            def drain(tag, scratch="lc"):
                out = (P_lc[0:32, 512:544] if scratch == "lc"
                       else P4[0:32, 1536:1568])
                E("tensor", lambda e, out=out: e.matmul(
                    out, t["ones_f32"][0:1, 0:32], t["ones_f32"][0:1, 0:32],
                    start=True, stop=True, skip_group_check=True),
                  inc=1, tag=tag)

            def ln_stats(pfx, p, src, w=(), junk=None, dump0=False):
                jk = junk if junk is not None else t["sq"]
                E("scalar", lambda e, jk=jk: e.activation(
                    out=jk[:], in_=src[:], func=AF.Identity,
                    accum_out=t["col0"][:]), waits=w, inc=1,
                  tag=f"s_{pfx}s{p}")
                E("scalar", lambda e: e.activation(
                    out=t["sq"][:], in_=src[:], func=AF.Square,
                    accum_out=t["col1"][:]), waits=w, inc=1,
                  tag=f"s_{pfx}q{p}")
                E("vector", lambda e: e.tensor_scalar(
                    out=t["col0"][:], in0=t["col0"][:], scalar1=1.0 / D,
                    scalar2=None, op0=ALU.mult),
                  waits=(f"s_{pfx}s{p}",), inc=0)
                E("vector", lambda e: e.tensor_scalar(
                    out=t["col1"][:], in0=t["col1"][:], scalar1=1.0 / D,
                    scalar2=EPS, op0=ALU.mult, op1=ALU.add),
                  waits=(f"s_{pfx}q{p}",), inc=0)
                E("vector", lambda e: e.tensor_tensor(
                    out=t["col2"][:], in0=t["col0"][:], in1=t["col0"][:],
                    op=ALU.mult), inc=0)
                vpad()
                E("vector", lambda e: e.tensor_tensor(
                    out=t["col1"][:], in0=t["col1"][:], in1=t["col2"][:],
                    op=ALU.subtract), inc=1, tag=f"v_{pfx}v{p}")
                E("scalar", lambda e: e.activation(
                    out=t["col1"][:], in_=t["col1"][:], func=AF.Sqrt),
                  waits=(f"v_{pfx}v{p}",), inc=1,
                  tag=f"s_{pfx}r{p}")
                E("vector", lambda e: e.reciprocal(out=t["col1"][:],
                                                   in_=t["col1"][:]),
                  waits=(f"s_{pfx}r{p}",), inc=0)
                vpad()
                E("vector", lambda e: e.scalar_tensor_tensor(
                    out=t["col2"][:], in0=t["col0"][:], scalar=-1.0,
                    in1=t["col1"][:], op0=ALU.mult, op1=ALU.mult),
                  inc=0)
                vpad()
                E("vector", lambda e: e.engine_nop(), inc=1,
                  tag=f"v_{pfx}n{p}")

            for k in consts:
                if k == "selrow":
                    continue
                E("sync", lambda e, k=k: e.dma_start(out=t[k][:],
                                                     in_=d_consts[k].ap()),
                  dma=True)
            E("sync", lambda e: e.dma_start(out=t["gam_sb"][:], in_=dgam),
              dma=True)
            E("sync", lambda e: e.dma_start(out=t["bet_sb"][:], in_=dbet),
              dma=True)
            E("sync", lambda e: e.dma_start(out=t["m64s"][:], in_=dmm), dma=True)
            E("sync", lambda e: e.dma_start(out=t["m64cs"][:], in_=dmc), dma=True)
            E("sync", lambda e: e.dma_start(out=t["b1raw"][:], in_=db1), dma=True)
            E("sync", lambda e: e.dma_start(out=t["b2raw"][:], in_=db2),
              dma=True, tag="c_const", tag_all=True)
            E("gpsimd", lambda e: e.dma_start(out=t["bf_sb"][:], in_=dbf),
              dma=True, tag="c_bf")

            def w_load(wap, dst, tag):
                src = bass.AP(wap.tensor, 0,
                              [[1024, 128], [128 * 1024, 8], [1, 1024]])
                E("gpsimd", lambda e, src=src, dst=dst: e.dma_start(
                    out=t[dst][:].rearrange("p (dc d) -> p dc d", dc=8),
                    in_=src), dma=True, tag=tag)

            w_load(dW["Wk"], "wA", "w_Wk")
            w_load(dW["Wv"], "wB", "w_Wv")
            w_load(dW["Wf"], "wC", "w_Wf")

            E("vector", lambda e: e.memset(t["zeroS"][:], 0.0), inc=1,
              tag="v_z0")
            E("gpsimd", lambda e: e.dma_start(out=d_sloc[0], in_=t["zeroS"][:]),
              waits=("v_z0",), dma=True)
            E("vector", lambda e: e.memset(t["mmx"][:, 0:1], -1e30), inc=0)
            E("vector", lambda e: e.memset(t["mmx"][:, 1:2], -1e30), inc=0)
            E("vector", lambda e: e.memset(t["Scur"][:], 0.0), inc=0)
            E("vector", lambda e: e.memset(t["wabs"][:], 0.0), inc=0)
            E("vector", lambda e: e.memset(t["onecol"][:], 1.0), inc=0)
            E("vector", lambda e: e.memset(t["zerocol"][:], 0.0), inc=0)
            E("vector", lambda e: e.memset(t["negMcol"][:], -MAGIC), inc=1,
              tag="v_init")

            E("tensor", lambda e: mm(e, P4[:, 0:512], t["ones_f32"][0:1, :],
                                     t["gam_sb"][0:1, 0:512], True, True),
              waits=("c_const",))
            E("tensor", lambda e: mm(e, P4[:, 512:1024], t["ones_f32"][0:1, :],
                                     t["gam_sb"][0:1, 512:1024], True, True))
            E("tensor", lambda e: mm(e, P4[:, 1024:1536], t["ones_f32"][0:1, :],
                                     t["bet_sb"][0:1, 0:512], True, True))
            E("tensor", lambda e: mm(e, P4[:, 1536:2048], t["ones_f32"][0:1, :],
                                     t["bet_sb"][0:1, 512:1024], True, True),
              inc=1, tag="t_gb")
            E("vector", lambda e: e.tensor_copy(out=t["gamma_bc"][:],
                                                in_=P4[:, 0:1024]),
              waits=("t_gb",), inc=0)
            E("vector", lambda e: e.tensor_copy(out=t["beta_bc"][:],
                                                in_=P4[:, 1024:2048]),
              inc=1, tag="v_gb")
            E("tensor", lambda e: e.transpose(P_s[:, 0:32], t["b1raw"][:],
                                              t["ident_f32"][0:32, 0:32]),
              waits=("c_const",), inc=1, tag="t_b1T")
            drain("d_b1T")
            E("vector", lambda e: e.tensor_copy(out=t["b1T"][:],
                                                in_=P_s[:, 0:32]),
              waits=("d_b1T",), inc=1, tag="v_b1T")
            E("tensor", lambda e: e.transpose(P_s[:, 32:40], t["b2raw"][:],
                                              t["ident_f32"][0:8, 0:8]),
              waits=("v_b1T",), inc=1, tag="t_b2T")
            drain("d_b2T")
            E("vector", lambda e: e.tensor_copy(out=t["b2T"][:],
                                                in_=P_s[:, 32:40]),
              waits=("d_b2T",), inc=1, tag="v_b2T")

            for tt in range(NTI):
                buf = t["xtm0"] if tt % 2 == 0 else t["xtm1"]
                wl = [f"t_xT{tt-2}"] if tt >= 2 else []
                E("gpsimd", lambda e, b=buf, tt=tt: e.dma_start(
                    out=b[:], in_=dx[tt * 128:(tt + 1) * 128, :]),
                  waits=wl, dma=True, tag=f"g_x{tt}")
                wl2 = [f"g_x{tt}", "c_const"]
                if tt >= 1:
                    wl2.append(f"v_xe{tt-1}")
                for dc in range(8):
                    E("tensor", lambda e, b=buf, dc=dc: e.transpose(
                        PKX[:, dc * 128:(dc + 1) * 128],
                        b[:, dc * 128:(dc + 1) * 128], t["ident_bf"][:]),
                      waits=(wl2 if dc == 0 else ()),
                      inc=(1 if dc == 7 else 0),
                      tag=(f"t_xT{tt}" if dc == 7 else None))
                drain(f"d_xT{tt}")
                E("vector", lambda e, tt=tt: e.tensor_copy(
                    out=t["AX"].ap()[:, tt * 1024:(tt + 1) * 1024],
                    in_=PKX),
                  waits=(f"d_xT{tt}",), inc=1, tag=f"v_xe{tt}")

            if dbg2:
                E("sync", lambda e: e.dma_start(out=d2_xf0,
                                                in_=t["AX"].ap()[:, 0:4096]),
                  waits=(f"v_xe{NTI-1}",), dma=True, tag="ydbg_xf0")
            for st in range(NST):
                kb = t["kbuf0"] if st % 2 == 0 else t["kbuf1"]
                for dc in range(8):
                    pb = P4[:, (dc % 4) * 512:(dc % 4) * 512 + 512]
                    wl = [f"v_xe{4*st+3}", "w_Wk",
                          "v_gb"]
                    if st * 8 + dc >= 4:
                        wl.append(f"v_ke{st*8+dc-4}")
                    for kk in range(8):
                        E("tensor", lambda e, pb=pb, kk=kk, dc=dc, st=st, wl=wl:
                          mm(e, pb[:],
                             t["wA"][:, kk * 1024 + dc * 128:
                                     kk * 1024 + dc * 128 + 128],
                             xfm_rhs(kk, st),
                             kk == 0, kk == 7),
                          waits=(wl if kk == 0 else ()),
                          inc=(1 if kk == 7 else 0),
                          tag=(f"t_k{st*8+dc}" if kk == 7 else None))
                    wke = [f"t_k{st*8+dc}"]
                    if st >= 2 and dc == 0:
                        wke.append(f"y_ks{st-2}")
                    E("vector", lambda e, pb=pb, dc=dc, kb=kb, wke=wke:
                      e.tensor_copy(
                        out=kb[:, dc * 512:(dc + 1) * 512], in_=pb[:]),
                      waits=wke, inc=1,
                      tag=f"v_ke{st*8+dc}")
                E("sync", lambda e, st=st, kb=kb: e.dma_start(
                    out=d_k[st], in_=kb[:]),
                  waits=(f"v_ke{st*8+7}",), dma=True,
                  tag=f"y_ks{st}")

            if phase_limit <= 0:
                return
            for p in range(NPAIR):
                tc = p * 128
                kfr = t["kfmr0"] if p % 2 == 0 else t["kfmr1"]
                src = bass.AP(d_k.tensor,
                              (p // 4) * 128 * 4096 + (p % 4) * 128,
                              [[4096, 128], [512, 8], [1, 128]])
                wl = [f"y_ks{p//4}"]
                if p >= 2:
                    wl.append(f"t_ktm{p-2}")
                E("sync", lambda e, kfr=kfr, src=src: e.dma_start(
                    out=kfr[:].rearrange("q (dc c) -> q dc c", dc=8), in_=src),
                  waits=wl, dma=True, tag=f"y_kfr{p}")

                wl = ["w_Wv", "w_Wf"]
                if p > 0:
                    wl.append(f"s_x3{p-1}")
                for nh in range(2):
                    pb = P4[:, nh * 512: nh * 512 + 512]
                    for kk in range(8):
                        E("tensor", lambda e, pb=pb, kk=kk, nh=nh, p=p, wl=wl:
                          mm(e, pb[:],
                             t["AX"].ap()[:, p * 1024 + kk * 128:
                                          p * 1024 + kk * 128 + 128],
                             t["wB"][:, kk * 1024 + nh * 512:
                                     kk * 1024 + nh * 512 + 512],
                             kk == 0, kk == 7),
                          waits=(wl if (nh == 0 and kk == 0) else ()),
                          inc=(1 if (nh == 1 and kk == 7) else 0),
                          tag=(f"t_v{p}" if (nh == 1 and kk == 7) else None))
                wtb2 = [f"v_ktm{p-1}"] if p > 0 else []
                for nh in range(2):
                    pb = P4[:, 1024 + nh * 512: 1024 + nh * 512 + 512]
                    for kk in range(8):
                        E("tensor", lambda e, pb=pb, kk=kk, nh=nh, p=p,
                          wtb2=wtb2:
                          mm(e, pb[:],
                             t["AX"].ap()[:, p * 1024 + kk * 128:
                                          p * 1024 + kk * 128 + 128],
                             t["wC"][:, kk * 1024 + nh * 512:
                                     kk * 1024 + nh * 512 + 512],
                             kk == 0, False),
                          waits=(wtb2 if (nh == 0 and kk == 0) else ()))
                    E("tensor", lambda e, pb=pb, nh=nh: mm(
                        e, pb[:], t["ones_bf"][0:1, :],
                        t["bf_sb"][0:1, nh * 512: nh * 512 + 512],
                        False, True),
                      waits=(("c_bf",)
                             if (p == 0 and nh == 0) else ()),
                      inc=(1 if nh == 1 else 0),
                      tag=(f"t_f{p}" if nh == 1 else None))
                wvv = [f"t_v{p}"]
                if p > 0:
                    wvv.append(f"v_wt{p-1}")
                E("scalar", lambda e: e.copy(out=t["vbuf"][:],
                                             in_=P4[:, 0:1024]),
                  waits=wvv, inc=1, tag=f"s_vev{p}")
                E("scalar", lambda e: e.activation(out=t["ebuf"][:],
                                                   in_=P4[:, 1024:2048],
                                                   func=AF.Exp, scale=-1.0),
                  waits=(f"t_f{p}",), inc=1, tag=f"s_fe{p}")
                wsp = [f"t_lcm{p-1}"] if p > 0 else []
                E("scalar", lambda e: e.activation(out=t["spbuf"][:],
                                                   in_=t["ebuf"][:],
                                                   func=AF.Ln,
                                                   bias=t["onecol"][:, 0:1],
                                                   scale=1.0),
                  waits=wsp, inc=1, tag=f"s_sp{p}")
                wsph = [f"s_sp{p}"]
                if p > 0:
                    wsph.append(f"t_D{2*p-1}")
                E("vector", lambda e: e.tensor_copy(out=t["sph"][:],
                                                    in_=t["spbuf"][:]),
                  waits=wsph, inc=0)
                E("vector", lambda e: e.scalar_tensor_tensor(
                    out=t["spl"][:], in0=t["sph"][:], scalar=-1.0,
                    in1=t["spbuf"][:], op0=ALU.mult, op1=ALU.add),
                  inc=1, tag=f"v_spl{p}")
                wlt = [f"s_vev{p}", f"s_sp{p}"]
                for nh in range(2):
                    E("tensor", lambda e, nh=nh, wlt=wlt: mm(
                        e, P4[:, nh * 512: nh * 512 + 512], t["trioA"][:],
                        t["spbuf"][:, nh * 512: nh * 512 + 512], True, True),
                      waits=(wlt if nh == 0 else ()))
                for nh in range(2):
                    E("tensor", lambda e, nh=nh: mm(
                        e, P4[:, 1024 + nh * 512: 1024 + nh * 512 + 512],
                        t["trioB"][:],
                        t["spbuf"][:, nh * 512: nh * 512 + 512], True, True),
                      inc=(1 if nh == 1 else 0),
                      tag=(f"t_tri{p}" if nh == 1 else None))
                g, q = p // 16, p % 16
                lcstop = (q == 15 or p == NPAIR - 1)
                for nh in range(2):
                    E("tensor", lambda e, nh=nh, g=g, q=q, lcstop=lcstop: mm(
                        e, P_lc[g * 32:(g + 1) * 32,
                                nh * 512: nh * 512 + 512],
                        t["triLc"][:, q * 32:(q + 1) * 32],
                        t["spbuf"][:, nh * 512: nh * 512 + 512],
                        q == 0, lcstop, tp=(0, g * 32), sg=True),
                      inc=(1 if nh == 1 else 0),
                      tag=(f"t_lcm{p}" if nh == 1 else None))
                ab = t["abuf0"] if p % 2 == 0 else t["abuf1"]
                wlx = [f"t_tri{p}"]
                if p >= 2:
                    wlx.append(f"y_at{p-2}")
                    wlx.append(f"y_at0{p-2}")
                E("scalar", lambda e, ab=ab: e.activation(
                    out=ab[:], in_=P4[:, 0:1024], func=AF.Exp),
                  waits=wlx, inc=1, tag=f"s_ae{p}")
                E("scalar", lambda e: e.activation(out=t["eibuf"][:],
                                                   in_=P4[:, 0:1024],
                                                   func=AF.Exp, scale=-1.0),
                  inc=0)
                E("scalar", lambda e: e.activation(out=t["ewbuf"][:],
                                                   in_=P4[:, 1024:2048],
                                                   func=AF.Exp),
                  inc=1, tag=f"s_x3{p}")
                vt = t["vtb0"] if p % 2 == 0 else t["vtb1"]
                wlv = [f"s_x3{p}"]
                if p >= 2:
                    wlv.append(f"y_vt{p-2}")
                E("vector", lambda e, vt=vt: e.tensor_tensor(
                    out=vt[:], in0=t["vbuf"][:], in1=t["eibuf"][:],
                    op=ALU.mult),
                  waits=wlv, inc=1, tag=f"v_vt{p}")
                wwt = [f"t_u{2*p-1}"] if p > 0 else []
                E("vector", lambda e: e.tensor_tensor(
                    out=t["wtb"][:], in0=t["vbuf"][:], in1=t["ewbuf"][:],
                    op=ALU.mult), waits=wwt, inc=1, tag=f"v_wt{p}")
                abn = "abuf0" if p % 2 == 0 else "abuf1"
                for hp2 in range(2):
                    asrc = bass.AP(t["AG2"].ap().tensor,
                                   B[abn][1] + 64 * hp2,
                                   [[G2C, 128], [128, 8], [1, 64]])
                    E("sync", lambda e, asrc=asrc, p=p, hp2=hp2: e.dma_start(
                        out=d_at[p][:, hp2 * 512:(hp2 + 1) * 512], in_=asrc),
                      waits=((f"s_ae{p}",) if hp2 == 0 else ()),
                      dma=True,
                      tag=(f"y_at{p}" if hp2 == 1 else f"y_at0{p}"))
                E("sync", lambda e, vt=vt, p=p: e.dma_start(out=d_vt[p],
                                                            in_=vt[:]),
                  waits=(f"v_vt{p}",), dma=True, tag=f"y_vt{p}")
                wlk = [f"y_kfr{p}", f"s_x3{p}"]
                if p > 0:
                    wlk.append(f"v_ktm{p-1}")
                for dc in range(8):
                    E("tensor", lambda e, dc=dc, kfr=kfr: e.transpose(
                        PKM[:, dc * 128:(dc + 1) * 128],
                        kfr[:, dc * 128:(dc + 1) * 128], t["ident_bf"][:]),
                      waits=(wlk if dc == 0 else ()),
                      inc=(1 if dc == 7 else 0),
                      tag=(f"t_ktm{p}" if dc == 7 else None))
                drain(f"d_ktm{p}", scratch="p4")
                wkm = [f"d_ktm{p}"]
                if p > 0:
                    wkm.append(f"t_u{2*p-1}")
                E("vector", lambda e: e.tensor_copy(out=t["ktm"][:],
                                                    in_=PKM),
                  waits=wkm, inc=1, tag=f"v_ktm{p}")

                for cpar in range(2):
                    ci = 2 * p + cpar
                    lha = t["negones"][:, cpar * 64:(cpar + 1) * 64]
                    wld = [f"v_spl{p}"]
                    if ci > 0:
                        wld.append(f"v_ch{ci-1}")
                    for par in range(2):
                        c1 = bass.AP(t["AG2"].ap().tensor, B["sph"][1] + par * 64,
                                     [[G2C, 128], [128, 8], [1, 64]])
                        c2 = bass.AP(t["AG2"].ap().tensor,
                                     B["spl"][1] + par * 64,
                                     [[G2C, 128], [128, 8], [1, 64]])
                        E("tensor", lambda e, lha=lha, c1=c1, par=par, wld=wld:
                          mm(e, P_b7[par * 64:(par + 1) * 64, :], lha, c1,
                             True, False, tp=(0, par * 64), sg=True),
                          waits=(wld if par == 0 else ()))
                        E("tensor", lambda e, lha=lha, c2=c2, par=par: mm(
                            e, P_b7[par * 64:(par + 1) * 64, :], lha, c2,
                            False, True, tp=(0, par * 64), sg=True),
                          inc=(1 if par == 1 else 0),
                          tag=(f"t_D{ci}" if par == 1 else None))
                    E("scalar", lambda e: e.activation(out=t["dbuf"][:],
                                                       in_=P_b7[:],
                                                       func=AF.Exp),
                      waits=(f"t_D{ci}",), inc=1,
                      tag=f"s_D{ci}")
                    wlu = [f"v_ktm{p}",
                           f"v_wt{p}"]
                    if ci > 0:
                        wlu.append(f"v_ch{ci-1}")
                    for h in range(H):
                        hp, hq = h % 2, h // 2
                        E("tensor", lambda e, h=h, hp=hp, hq=hq, cpar=cpar,
                          wlu=wlu: mm(
                            e, P_s[hp * 64:(hp + 1) * 64,
                                   hq * 64:(hq + 1) * 64],
                            t["ktm"][cpar * 64:(cpar + 1) * 64,
                                     h * 64:(h + 1) * 64],
                            t["wtb"][cpar * 64:(cpar + 1) * 64,
                                     h * 64:(h + 1) * 64],
                            True, True, tp=(cpar * 64, hp * 64), sg=True),
                          waits=(wlu if h == 0 else ()),
                          inc=(1 if h == 15 else 0),
                          tag=(f"t_u{ci}" if h == 15 else None))
                    E("vector", lambda e: e.tensor_tensor(
                        out=t["tmpS"][:], in0=t["Scur"][:], in1=t["dbuf"][:],
                        op=ALU.mult),
                      waits=(f"s_D{ci}",), inc=0)
                    E("vector", lambda e: e.tensor_tensor(
                        out=t["Scur"][:], in0=t["tmpS"][:], in1=P_s[:],
                        op=ALU.add),
                      waits=(f"t_u{ci}",), inc=1,
                      tag=f"v_ch{ci}")
                    slb = t["slocbf0"] if ci % 2 == 0 else t["slocbf1"]
                    wls = [f"v_ch{ci}"]
                    if ci >= 2:
                        wls.append(f"y_sl{ci-2}")
                    E("scalar", lambda e, slb=slb: e.copy(out=slb[:],
                                                          in_=t["Scur"][:]),
                      waits=wls, inc=1, tag=f"s_sl{ci}")
                    E("sync", lambda e, slb=slb, ci=ci: e.dma_start(
                        out=d_sloc[ci + 1], in_=slb[:]),
                      waits=(f"s_sl{ci}",), dma=True,
                      tag=f"y_sl{ci}")

            if phase_limit <= 1:
                return
            E("sync", lambda e: e.dma_start(out=t["selrow"][:],
                                            in_=d_consts["selrow"].ap()),
              waits=(f"t_u{NCH-1}",), dma=True, tag="y_selr")
            if NCH < 64:
                E("vector", lambda e: e.memset(t["Lc_sb"][0:64, :], 0.0),
                  waits=(f"v_ch{NCH-1}",), inc=1, tag="v_lcz")
            E("scalar", lambda e: e.copy(out=t["Lc_sb"][0:NCH, :],
                                         in_=P_lc[0:NCH, :]),
              waits=(f"t_u{NCH-1}",)
                    + (("v_lcz",) if NCH < 64 else ()),
              inc=1, tag="s_lcev")
            for nh in range(2):
                E("tensor", lambda e, nh=nh: mm(
                    e, P_lc[0:65, nh * 512: nh * 512 + 512], t["triA65"][:],
                    t["Lc_sb"][:, nh * 512: nh * 512 + 512], True, True,
                    sg=True),
                  waits=(("s_lcev",) if nh == 0 else ()),
                  inc=(1 if nh == 1 else 0),
                  tag=("t_triA" if nh == 1 else None))
            E("scalar", lambda e: e.activation(out=t["e2r"][:],
                                               in_=P_lc[0:64, :],
                                               func=AF.Exp),
              waits=("t_triA",), inc=0)
            E("scalar", lambda e: e.copy(out=t["acu"][64:65, :],
                                         in_=P_lc[64:65, :]),
              inc=1, tag="s_acu")
            E("scalar", lambda e: e.activation(out=t["e1r"][:],
                                               in_=t["Lc_sb"][:],
                                               func=AF.Exp, scale=0.5),
              inc=1, tag="s_e1r")
            for par in range(2):
                src = bass.AP(t["AF"].ap().tensor, 64 * FAC + B["acu"][1] + par * 64,
                              [[FAC, 1], [128, 8], [1, 64]])
                E("tensor", lambda e, par=par, src=src: mm(
                    e, P_s[par * 64:(par + 1) * 64, :],
                    t["ones_f32"][64:65, 0:64], src, True, True,
                    tp=(64, par * 64), sg=True),
                  waits=(("s_acu",) if par == 0 else ()),
                  inc=(1 if par == 1 else 0),
                  tag=("t_dtb" if par == 1 else None))
            E("scalar", lambda e: e.activation(out=t["dex_st"][:], in_=P_s[:],
                                               func=AF.Exp),
              waits=("t_dtb",), inc=1, tag="s_dex")
            E("sync", lambda e: e.dma_start(out=d_agin[0:128, :],
                                            in_=t["Scur"][:]),
              waits=(f"v_ch{NCH-1}",), dma=True, tag="y_agin0")
            E("sync", lambda e: e.dma_start(out=d_agin[128:256, :],
                                            in_=t["dex_st"][:]),
              waits=("s_dex",), dma=True, tag="y_agin")
            E("gpsimd", lambda e: e.collective_compute(
                "AllGather", ALU.bypass, replica_groups=[list(range(8))],
                ins=[d_agin.opt()],
                outs=[d_agout.rearrange("r p c -> (r p) c").opt()]),
              waits=("y_agin", "y_agin0"), cc=True, tag="cc_ag")
            if dbg2:
                E("sync", lambda e: e.dma_start(out=d2_xf1,
                                                in_=t["AX"].ap()[:, 0:4096]),
                  waits=(f"y_sl{NCH-1}",), dma=True, tag="ydbg_xf1")
            E("vector", lambda e: e.memset(t["S_in"][:], 0.0), inc=1,
              tag="v_si0")
            for r in range(8):
                wsg = ["cc_ag",
                       (f"v_fold{r-1}" if r > 0 else "v_si0")]
                E("sync", lambda e, r=r: e.dma_start(out=t["Sg"][:],
                                                     in_=d_agout[r, 0:128, :]),
                  waits=wsg, dma=True)
                E("sync", lambda e, r=r: e.dma_start(
                    out=t["Dg"][:], in_=d_agout[r, 128:256, :]),
                  dma=True, tag=f"y_dg{r}")
                E("vector", lambda e, r=r: e.tensor_scalar(
                    out=t["cfold"][:], in0=t["Dg"][:],
                    scalar1=t["m64s"][:, r:r+1], scalar2=None, op0=ALU.mult),
                  waits=(f"y_dg{r}",), inc=0)
                E("vector", lambda e, r=r: e.tensor_scalar(
                    out=t["cfold"][:], in0=t["cfold"][:],
                    scalar1=t["m64cs"][:, r:r+1], scalar2=None, op0=ALU.add),
                  inc=0)
                E("vector", lambda e: e.tensor_tensor(
                    out=t["S_in"][:], in0=t["S_in"][:], in1=t["cfold"][:],
                    op=ALU.mult), inc=0)
                E("vector", lambda e, r=r: e.tensor_scalar(
                    out=t["Sg"][:], in0=t["Sg"][:],
                    scalar1=t["m64s"][:, r:r+1], scalar2=None, op0=ALU.mult),
                  inc=0)
                E("vector", lambda e: e.tensor_tensor(
                    out=t["S_in"][:], in0=t["S_in"][:], in1=t["Sg"][:],
                    op=ALU.add), inc=1, tag=f"v_fold{r}")
            if dbg2:
                E("sync", lambda e: e.dma_start(out=d2_xf2,
                                                in_=t["AX"].ap()[:, 0:4096]),
                  waits=("v_fold7",), dma=True, tag="ydbg_xf2")
                E("sync", lambda e: e.dma_start(
                    out=d2_er, in_=t["AG1"].ap()[:, 8192:10240]),
                  waits=("s_e1r",), dma=True, tag="ydbg_er")
                E("sync", lambda e: e.dma_start(out=d2_sp, in_=t["SP"][:]),
                  dma=True, tag="ydbg_sp")
                E("sync", lambda e: e.dma_start(
                    out=d2_lc, in_=t["Lc_sb"]),
                  dma=True, tag="ydbg_lc")
            w_load(dW["Wq"], "wA", "w_Wq")

            if phase_limit <= 2:
                return
            for p in range(NPAIR):
                if phase_limit <= 2.0 + p * 0.1:
                    return
                st = p // 4
                if p % 4 == 0:
                    qb = t["qbuf0"]
                    for dc in range(8):
                        pb = P_lc[:, (dc % 2) * 512:(dc % 2) * 512 + 512]
                        wl = ["w_Wq", "s_acu",
                              "s_e1r"]
                        if st * 8 + dc >= 2:
                            wl.append(f"v_qe{st*8+dc-2}")
                        for kk in range(8):
                            E("tensor", lambda e, pb=pb, kk=kk, dc=dc, st=st,
                              wl=wl: mm(
                                e, pb[:],
                                t["wA"][:, kk * 1024 + dc * 128:
                                        kk * 1024 + dc * 128 + 128],
                                xfm_rhs(kk, st),
                                kk == 0, kk == 7),
                              waits=(wl if kk == 0 else ()),
                              inc=(1 if kk == 7 else 0),
                              tag=(f"t_q{st*8+dc}" if kk == 7 else None))
                        E("vector", lambda e, pb=pb, dc=dc, qb=qb:
                          e.tensor_copy(out=qb[:, dc * 512:(dc + 1) * 512],
                                        in_=pb[:]),
                          waits=(f"t_q{st*8+dc}",), inc=1,
                          tag=f"v_qe{st*8+dc}")
                    kb2 = t["kbuf0"] if st % 2 == 0 else t["kbuf1"]
                    wlk = []
                    if p >= 8:
                        wlk.append(f"t_sc{2*(p-5)+1}")
                    E("sync", lambda e, st=st, kb2=kb2: e.dma_start(
                        out=kb2[:], in_=d_k[st]),
                      waits=wlk, dma=True, tag=f"y_kr{st}")
                if p == 0 and phase_limit <= 2.01:
                    return
                if dbg2 and p == 0:
                    E("sync", lambda e: e.dma_start(out=d2_qb,
                                                    in_=t["qbuf0"][:]),
                      waits=(f"v_qe{7}",), dma=True, tag="ydbg_qb")
                    E("sync", lambda e: e.dma_start(out=d2_wa,
                                                    in_=t["wA"][:]),
                      waits=("w_Wq",), dma=True, tag="ydbg_wa")
                    E("sync", lambda e: e.dma_start(
                        out=d2_xf, in_=t["AX"].ap()[:, 0:4096]),
                      dma=True, tag="ydbg_xf")
                qb = t["qbuf0"]
                kb2 = t["kbuf0"] if st % 2 == 0 else t["kbuf1"]
                vtr = t["vtr0"] if p % 2 == 0 else t["vtr1"]
                atr = t["atr0"] if p % 2 == 0 else t["atr1"]
                wlr = [f"v_ome{p-2}"] if p >= 2 else []
                E("sync", lambda e, vtr=vtr, p=p: e.dma_start(out=vtr[:],
                                                              in_=d_vt[p]),
                  waits=wlr, dma=True, tag=f"y_vtr{p}")
                E("sync", lambda e, atr=atr, p=p: e.dma_start(out=atr[:],
                                                              in_=d_at[p]),
                  dma=True, tag=f"y_atr{p}")
                for cpar in range(2):
                    if p == 0 and phase_limit <= 2.02 + cpar * 0.03:
                        return
                    ci = 2 * p + cpar
                    cl = p % 4
                    slr = t["slocr0"] if ci % 2 == 0 else t["slocr1"]
                    wls = [f"v_ss{ci-2}"] if ci >= 2 else []
                    E("sync", lambda e, slr=slr, ci=ci: e.dma_start(
                        out=slr[:], in_=d_sloc[ci]),
                      waits=wls, dma=True, tag=f"y_slr{ci}")
                    sel = t["selrow"][:, ci * 64:(ci + 1) * 64]
                    wle = ["s_e1r", "s_dex",
                           "y_selr"]
                    if ci > 0:
                        wle.append(f"v_ss{ci-1}")
                    for par in range(2):
                        e1src = bass.AP(t["AG1"].ap().tensor,
                                        8192 + par * 64,
                                        [[G1C, 64], [128, 8], [1, 64]])
                        E("tensor", lambda e, sel=sel, src=e1src, par=par,
                          wle=wle: mm(
                            e, P_s[par * 64:(par + 1) * 64, :], sel, src,
                            True, True, tp=(0, par * 64), sg=True),
                          waits=(wle if par == 0 else ()))
                    for par in range(2):
                        e2src = bass.AP(t["AG1"].ap().tensor,
                                        9216 + par * 64,
                                        [[G1C, 64], [128, 8], [1, 64]])
                        E("tensor", lambda e, sel=sel, src=e2src, par=par: mm(
                            e, P_b7[par * 64:(par + 1) * 64, :], sel, src,
                            True, True, tp=(0, par * 64), sg=True),
                          inc=(1 if par == 1 else 0),
                          tag=(f"t_e12{ci}" if par == 1 else None))
                    wfx = [f"t_e12{ci}",
                           f"y_slr{ci}", "v_fold7"]
                    if ci > 0:
                        wfx.append(f"t_ii{ci-1}")
                    if dbg2 and ci == 1:
                        wfx.append("ydbg_ssb0")
                    E("vector", lambda e, slr=slr, wfx=wfx: e.tensor_tensor(
                        out=t["ssA"][:], in0=slr[:], in1=P_s[:], op=ALU.mult),
                      waits=wfx, inc=0)
                    E("vector", lambda e: e.tensor_tensor(
                        out=t["coef"][:], in0=t["S_in"][:], in1=P_b7[:],
                        op=ALU.mult), inc=0)
                    E("vector", lambda e: e.tensor_tensor(
                        out=t["SSb"][:], in0=t["ssA"][:], in1=t["coef"][:],
                        op=ALU.add), inc=1, tag=f"v_ss{ci}")
                    if dbg2 and ci <= 1:
                        E("sync", lambda e, ci=ci: e.dma_start(
                            out=d2_ssb[ci], in_=t["SSb"][:]),
                          waits=(f"v_ss{ci}",), dma=True,
                          tag=f"ydbg_ssb{ci}")
                    if p == 0 and phase_limit <= 2.03 + cpar * 0.03:
                        return
                    wsc = [f"y_kr{st}",
                           f"v_qe{st*8+7}"]
                    if ci == 0:
                        wsc.append(f"s_x3{NPAIR-1}")
                    else:
                        wsc.append(f"v_stm{ci-1}")
                    if cpar == 0 and p >= 1:
                        wsc.append(f"s_oi{p-1}")
                    if (ser & 1) and cpar == 0 and p >= 1:
                        wsc.append(f"y_gn{p-1}")
                    for h in range(H):
                        hp, hq = h % 2, h // 2
                        base = hq * 512 + cl * 128 + cpar * 64
                        cn = hp * 512 + hq * 64
                        E("tensor", lambda e, hp=hp, base=base, cn=cn,
                          cpar=cpar, kb2=kb2, qb=qb, wsc=wsc: mm(
                            e,
                            P4[cpar * 64:(cpar + 1) * 64, cn:cn + 64],
                            kb2[hp * 64:(hp + 1) * 64, base:base + 64],
                            qb[hp * 64:(hp + 1) * 64, base:base + 64],
                            True, True, tp=(hp * 64, cpar * 64), sg=True),
                          waits=(wsc if h == 0 else ()),
                          inc=(1 if h == 15 else 0),
                          tag=(f"t_sc{ci}" if h == 15 else None))
                    mrep = bass.AP(t["maskc"].ap().tensor, 0,
                                   [[64, 64], [0, 16], [1, 64]])
                    E("vector", lambda e, cpar=cpar, mrep=mrep:
                      e.tensor_tensor(
                        out=t["stm"][cpar * 64:(cpar + 1) * 64, :]
                            .rearrange("q (h c) -> q h c", h=16),
                        in0=P4[cpar * 64:(cpar + 1) * 64, 0:1024]
                            .rearrange("q (h c) -> q h c", h=16),
                        in1=mrep, op=ALU.mult),
                      waits=(f"t_sc{ci}",)
                            + (("ydbg_stm1",) if dbg2 and ci == 2 else ()),
                      inc=1,
                      tag=f"v_stm{ci}")
                    if dbg2 and ci <= 1:
                        E("sync", lambda e, ci=ci, cpar=cpar: e.dma_start(
                            out=d2_stm[ci, cpar * 64:(cpar + 1) * 64, :],
                            in_=t["stm"][cpar * 64:(cpar + 1) * 64, :]),
                          waits=(f"v_stm{ci}",), dma=True,
                          tag=f"ydbg_stm{ci}")
                    if p == 0 and phase_limit <= 2.04 + cpar * 0.03:
                        return
                    wii = [f"v_stm{ci}",
                           f"v_ss{ci}",
                           f"y_vtr{p}"]
                    if cpar == 0 and p >= 1:
                        wii.append(f"v_oma{p-1}")
                    for h in range(H):
                        hp, hq = h % 2, h // 2
                        base = hq * 512 + cl * 128 + cpar * 64
                        cn = hp * 512 + hq * 64
                        E("tensor", lambda e, cn=cn, h=h, cpar=cpar, vtr=vtr,
                          wii=wii: mm(
                            e,
                            P4[cpar * 64:(cpar + 1) * 64,
                               1024 + cn:1024 + cn + 64],
                            t["stm"][cpar * 64:(cpar + 1) * 64,
                                     cn:cn + 64],
                            vtr[cpar * 64:(cpar + 1) * 64,
                                h * 64:(h + 1) * 64],
                            True, True,
                            tp=(cpar * 64, cpar * 64), sg=True),
                          waits=(wii if h == 0 else ()))
                        E("tensor", lambda e, cn=cn, hp=hp, hq=hq, base=base,
                          cpar=cpar, qb=qb: mm(
                            e,
                            P4[cpar * 64:(cpar + 1) * 64, cn:cn + 64],
                            qb[hp * 64:(hp + 1) * 64, base:base + 64],
                            t["SSb"][hp * 64:(hp + 1) * 64,
                                     hq * 64:(hq + 1) * 64],
                            True, True,
                            tp=(hp * 64, cpar * 64), sg=True),
                          inc=(1 if h == 15 else 0),
                          tag=(f"t_ii{ci}" if h == 15 else None))
                if p == 0 and phase_limit <= 2.08:
                    return
                E("scalar", lambda e: e.copy(out=t["silb"][:],
                                             in_=P4[:, 0:1024]),
                  waits=(f"t_ii{2*p+1}",), inc=1, tag=f"s_oi{p}")
                E("vector", lambda e: e.tensor_tensor(
                    out=t["om"][:], in0=t["silb"][:], in1=P4[:, 1024:2048],
                    op=ALU.add),
                  waits=(f"s_oi{p}",)
                        + (("ydbg_om",) if dbg2 and p == 1 else ()),
                  inc=1, tag=f"v_oma{p}")
                E("vector", lambda e, atr=atr: e.tensor_tensor(
                    out=t["om"][:], in0=t["om"][:], in1=atr[:],
                    op=ALU.mult),
                  waits=(f"y_atr{p}",),
                  inc=1, tag=f"v_ome{p}")
                if dbg2 and p == 0:
                    E("vector", lambda e: e.tensor_copy(out=t["x1t"][:],
                                                        in_=t["om"][:]),
                      inc=1, tag="v_omsnap")
                    E("sync", lambda e: e.dma_start(out=d2_om2,
                                                    in_=t["x1t"][:]),
                      waits=("v_omsnap",), dma=True, tag="ydbg_om2")
                    E("sync", lambda e: e.dma_start(out=d2_om,
                                                    in_=t["om"][:]),
                      waits=("v_ome0",), dma=True, tag="ydbg_om")
                if p == 0 and phase_limit <= 2.085:
                    return
                ln_stats("g", p, t["om"], w=[f"v_ome{p}"],
                         dump0=(dbg2 and p == 0))
                if dbg2 and p == 0:
                    E("sync", lambda e: e.dma_start(
                        out=d2_cols[:, 0:1], in_=t["col1"][:],
                        allow_slow_non_contiguous=True),
                      waits=("v_gn0",), dma=True, tag="ydbg_c1")
                    E("sync", lambda e: e.dma_start(
                        out=d2_cols[:, 1:2], in_=t["col2"][:],
                        allow_slow_non_contiguous=True),
                      dma=True, tag="ydbg_c2")
                if p == 0 and phase_limit <= 2.09:
                    return
                E("vector", lambda e: e.tensor_scalar(
                    out=t["gnorm"][:], in0=t["om"][:],
                    scalar1=t["col1"][:, 0:1], scalar2=t["col2"][:, 0:1],
                    op0=ALU.mult, op1=ALU.add),
                  waits=(f"v_gn{p}",)
                        + (("ydbg_c2",) if dbg2 and p == 0 else ()),
                  inc=1, tag=f"s_gn{p}")
                gnb = t["gnr0"] if p % 2 == 0 else t["gnr1"]
                E("vector", lambda e: e.tensor_tensor(
                    out=t["gnorm"][:], in0=t["gnorm"][:],
                    in1=t["gamma_bc"][:], op=ALU.mult),
                  waits=(f"s_gn{p}",), inc=0)
                if p == 0 and phase_limit <= 2.095:
                    return
                wgn = [f"y_gn{p-2}"] if p >= 2 else []
                E("vector", lambda e, gnb=gnb, wgn=wgn: e.tensor_tensor(
                    out=gnb[:], in0=t["gnorm"][:], in1=t["beta_bc"][:],
                    op=ALU.add), waits=wgn, inc=1, tag=f"v_gnf{p}")
                E("sync", lambda e, gnb=gnb, p=p: e.dma_start(out=d_gn[p],
                                                              in_=gnb[:]),
                  waits=(f"v_gnf{p}",), dma=True,
                  tag=f"y_gn{p}")
            src_g = bass.AP(dW["Wg"].tensor, 0,
                            [[1024, 128], [128 * 1024, 8], [1, 1024]])
            E("gpsimd", lambda e: e.dma_start(
                out=t["wB"][:].rearrange("p (dc d) -> p dc d", dc=8),
                in_=src_g),
              waits=(f"t_e12{NCH-1}",), dma=True, tag="w_Wg")
            w_load(dW["Wout"], "wC", "w_Wout")

            if phase_limit <= 3:
                return
            for p in range(NPAIR):
                tc = p * 128
                wl = ["w_Wg", "w_Wout",
                      f"v_stm{NCH-1}"]
                if p > 0:
                    wl.append(f"v_gf{p-1}")
                if (ser & 2) and p > 0:
                    wl.append(f"y_x1{p-1}")
                for nh in range(2):
                    pb = P4[:, nh * 512: nh * 512 + 512]
                    for kk in range(8):
                        E("tensor", lambda e, pb=pb, kk=kk, nh=nh, p=p,
                          wl=wl: mm(
                            e, pb[:],
                            t["AX"].ap()[:, p * 1024 + kk * 128:
                                         p * 1024 + kk * 128 + 128],
                            t["wB"][:, kk * 1024 + nh * 512:
                                    kk * 1024 + nh * 512 + 512],
                            kk == 0, kk == 7),
                          waits=(wl if (nh == 0 and kk == 0) else ()),
                          inc=(1 if (nh == 1 and kk == 7) else 0),
                          tag=(f"t_g{p}" if (nh == 1 and kk == 7) else None))
                E("scalar", lambda e: e.activation(out=t["silb"][:],
                                                   in_=P4[:, 0:1024],
                                                   func=AF.Silu),
                  waits=(f"t_g{p}",), inc=1, tag=f"s_sil{p}")
                gnb = t["gnr0"] if p % 2 == 0 else t["gnr1"]
                wlg = [f"y_gn{NPAIR-1}"]
                if p >= 2:
                    wlg.append(f"v_gt{p-2}")
                E("sync", lambda e, gnb=gnb, p=p: e.dma_start(out=gnb[:],
                                                              in_=d_gn[p]),
                  waits=wlg, dma=True, tag=f"y_gnr{p}")
                E("vector", lambda e, gnb=gnb: e.tensor_tensor(
                    out=t["gated"][:], in0=t["silb"][:], in1=gnb[:],
                    op=ALU.mult),
                  waits=(f"s_sil{p}", f"y_gnr{p}"),
                  inc=1, tag=f"v_gt{p}")
                for dc in range(8):
                    E("tensor", lambda e, dc=dc: e.transpose(
                        PKG[:, dc * 128:(dc + 1) * 128],
                        t["gated"][:, dc * 128:(dc + 1) * 128],
                        t["ident_bf"][:]),
                      waits=((f"v_gt{p}",) if dc == 0 else ()),
                      inc=(1 if dc == 7 else 0),
                      tag=(f"t_gT{p}" if dc == 7 else None))
                drain(f"d_gT{p}")
                E("vector", lambda e: e.tensor_copy(out=t["gfm"][:],
                                                    in_=PKG),
                  waits=(f"d_gT{p}",), inc=1, tag=f"v_gf{p}")
                wlo = [f"v_gf{p}"]
                if p == 0:
                    wlo.append(f"v_ome{NPAIR-1}")
                else:
                    wlo.append(f"v_r1{p-1}")
                for nh in range(2):
                    pb = P4[:, 1024 + nh * 512: 1024 + nh * 512 + 512]
                    for kk in range(8):
                        E("tensor", lambda e, pb=pb, kk=kk, nh=nh, wlo=wlo:
                          mm(e, pb[:],
                             t["gfm"][:, kk * 128: kk * 128 + 128],
                             t["wC"][:, kk * 1024 + nh * 512:
                                     kk * 1024 + nh * 512 + 512],
                             kk == 0, kk == 7),
                          waits=(wlo if (nh == 0 and kk == 0) else ()),
                          inc=(1 if (nh == 1 and kk == 7) else 0),
                          tag=(f"t_o{p}" if (nh == 1 and kk == 7) else None))
                xt = t["xt0"] if p % 2 == 0 else t["xt1"]
                wxr = [f"v_r1{p-2}"] if p >= 2 else \
                    [f"v_gnf{NPAIR-1}"]
                E("sync", lambda e, xt=xt, tc=tc: e.dma_start(
                    out=xt[:], in_=dx[tc:tc + 128, :]),
                  waits=wxr, dma=True, tag=f"y_xr{p}")
                E("vector", lambda e, xt=xt: e.tensor_tensor(
                    out=t["r1"][:], in0=P4[:, 1024:2048], in1=xt[:],
                    op=ALU.add),
                  waits=(f"t_o{p}", f"y_xr{p}"),
                  inc=1, tag=f"v_r1{p}")
                ln_stats("l", p, t["r1"], w=[f"v_r1{p}"])
                wlx1 = [f"v_ln{p}"]
                if p >= 1:
                    wlx1.append(f"y_x1{p-1}")
                if dbg2 and p == 0:
                    wlx1.append("ydbg_om2")
                E("vector", lambda e: e.tensor_scalar(
                    out=t["x1t"][:], in0=t["r1"][:],
                    scalar1=t["col1"][:, 0:1], scalar2=t["col2"][:, 0:1],
                    op0=ALU.mult, op1=ALU.add),
                  waits=wlx1, inc=1, tag=f"s_x1{p}")
                E("sync", lambda e, p=p: e.dma_start(out=d_x1[p],
                                                     in_=t["x1t"][:]),
                  waits=(f"s_x1{p}",), dma=True,
                  tag=f"y_x1{p}")
                E("vector", lambda e: e.tensor_scalar(
                    out=t["sq"][:], in0=t["x1t"][:], scalar1=0.0,
                    scalar2=None, op0=ALU.add, op1=ALU.max,
                    accum_out=t["col3"][:]),
                  waits=(f"s_x1{p}",), inc=0)
                E("vector", lambda e: e.tensor_scalar(
                    out=t["sq"][:], in0=t["x1t"][:], scalar1=-1.0,
                    scalar2=None, op0=ALU.mult, op1=ALU.max,
                    accum_out=t["col2"][:]),
                  inc=0)
                E("vector", lambda e: e.tensor_tensor(
                    out=t["mmx"][:, 0:1], in0=t["mmx"][:, 0:1],
                    in1=t["col3"][:], op=ALU.max), inc=0)
                E("vector", lambda e: e.tensor_tensor(
                    out=t["mmx"][:, 1:2], in0=t["mmx"][:, 1:2],
                    in1=t["col2"][:], op=ALU.max), inc=1, tag=f"v_mm{p}")

            E("vector", lambda e: e.tensor_scalar(
                out=t["tpad"][:, 0:1], in0=t["mmx"][:, 0:1], scalar1=0.0,
                scalar2=None, op0=ALU.add),
              waits=(f"v_mm{NPAIR-1}",), inc=0)
            E("vector", lambda e: e.tensor_scalar(
                out=t["tpad"][:, 1:2], in0=t["mmx"][:, 1:2], scalar1=0.0,
                scalar2=None, op0=ALU.add), inc=1, tag="v_tp1")
            E("tensor", lambda e: e.transpose(P_s[0:2, 0:128], t["tpad"][:],
                                              t["ident_f32"][:]),
              waits=("v_tp1", f"v_ss{NCH-1}"),
              inc=1, tag="t_tp1")
            drain("d_tp1")
            E("vector", lambda e: e.tensor_copy(out=t["mm2"][:],
                                                in_=P_s[0:2, 0:128]),
              waits=("d_tp1",), inc=0)
            vpad()
            E("vector", lambda e: e.tensor_scalar(
                out=t["sq"][0:2, 0:128], in0=t["mm2"][:], scalar1=0.0,
                scalar2=None, op0=ALU.add, op1=ALU.max,
                accum_out=t["mm2c"][:]),
              inc=1, tag="v_mmf")
            E("sync", lambda e: e.dma_start(
                out=d_mm1i.rearrange("(a b) -> a b", b=1), in_=t["mm2c"][:]),
              waits=("v_mmf",), dma=True, tag="y_mm1")
            E("gpsimd", lambda e: e.collective_compute(
                "AllReduce", ALU.max, replica_groups=[list(range(8))],
                ins=[d_mm1i.opt()], outs=[d_mm1o.opt()]),
              waits=("y_mm1",), cc=True, tag="cc_mm1")

            if phase_limit <= 4:
                return
            for i in range(32):
                wt = t["wtile0"] if i % 2 == 0 else t["wtile1"]
                wlw = [f"v_wa{i-2}"] if i >= 2 else []
                E("sync", lambda e, wt=wt, i=i: e.dma_start(
                    out=wt[:], in_=dW1[i * 128:(i + 1) * 128, :]),
                  waits=wlw, dma=True, tag=f"y_w1a{i}")
                E("vector", lambda e, wt=wt: e.tensor_scalar(
                    out=t["sq"][:], in0=wt[:], scalar1=0.0, scalar2=None,
                    op0=ALU.add, op1=ALU.max, accum_out=t["col3"][:]),
                  waits=(f"y_w1a{i}",), inc=0)
                E("vector", lambda e, wt=wt: e.tensor_scalar(
                    out=t["sq"][:], in0=wt[:], scalar1=-1.0, scalar2=None,
                    op0=ALU.mult, op1=ALU.max, accum_out=t["col2"][:]),
                  inc=0)
                E("vector", lambda e: e.tensor_tensor(
                    out=t["wabs"][:], in0=t["wabs"][:], in1=t["col3"][:],
                    op=ALU.max), inc=0)
                E("vector", lambda e: e.tensor_tensor(
                    out=t["wabs"][:], in0=t["wabs"][:], in1=t["col2"][:],
                    op=ALU.max), inc=1, tag=f"v_wa{i}")
            vpad()
            E("vector", lambda e: e.tensor_copy(out=t["tpad"][:, 0:1],
                                                in_=t["wabs"][:]),
              waits=("v_wa31",), inc=0)
            E("vector", lambda e: e.tensor_copy(out=t["tpad"][:, 1:2],
                                                in_=t["wabs"][:]),
              inc=1, tag="v_wtp")
            E("tensor", lambda e: e.transpose(P_s[0:2, 0:128], t["tpad"][:],
                                              t["ident_f32"][:]),
              waits=("v_wtp", "v_mmf"),
              inc=1, tag="t_wtp")
            drain("d_wtp")
            E("vector", lambda e: e.tensor_copy(out=t["mm2"][:],
                                                in_=P_s[0:2, 0:128]),
              waits=("d_wtp",), inc=0)
            vpad()
            E("vector", lambda e: e.tensor_scalar(
                out=t["sq"][0:1, 0:128], in0=t["mm2"][0:1, :], scalar1=0.0,
                scalar2=None, op0=ALU.add, op1=ALU.max,
                accum_out=t["sc"][0:1, SW1:SW1+1]), inc=0)
            vpad()
            E("vector", lambda e: e.tensor_scalar(
                out=t["sc"][0:1, SW1:SW1+1], in0=t["sc"][0:1, SW1:SW1+1],
                scalar1=1.0 / 127.0, scalar2=1e-8, op0=ALU.mult,
                op1=ALU.max), inc=0)
            vpad()
            E("vector", lambda e: e.reciprocal(
                out=t["sc"][0:1, INVSW1:INVSW1+1],
                in_=t["sc"][0:1, SW1:SW1+1]), inc=0)
            vpad()
            E("vector", lambda e: e.engine_nop(), inc=1, tag="v_sw1")
            E("tensor", lambda e: mm(e, P_b7[:, 0:1], t["ones_f32"][0:1, :],
                                     t["sc"][0:1, INVSW1:INVSW1+1],
                                     True, True),
              waits=("v_sw1",), inc=1, tag="t_bw1")
            E("vector", lambda e: e.tensor_copy(out=t["scbc"][:, 0:1],
                                                in_=P_b7[:, 0:1]),
              waits=("t_bw1",), inc=1, tag="v_bw1")
            vpad()
            for i in range(32):
                wt = t["wtile0"] if i % 2 == 0 else t["wtile1"]
                wlw = ["v_bw1"]
                if i >= 2:
                    wlw.append(f"v_w1e{i-2}")
                wldb = [f"v_wa31"]
                if i >= 2:
                    wldb.append(f"v_w1q{i-2}")
                E("sync", lambda e, wt=wt, i=i: e.dma_start(
                    out=wt[:], in_=dW1[i * 128:(i + 1) * 128, :]),
                  waits=wldb, dma=True,
                  tag=f"y_w1b{i}")
                if i == 0:
                    wlw.append(f"s_x1{NPAIR-1}")
                E("vector", lambda e, wt=wt, wlw=wlw: e.tensor_scalar(
                    out=t["sq"][:], in0=wt[:], scalar1=t["scbc"][:, 0:1],
                    scalar2=MAGIC, op0=ALU.mult, op1=ALU.add),
                  waits=wlw + [f"y_w1b{i}"], inc=0)
                E("vector", lambda e: e.tensor_scalar(
                    out=t["sq"][:], in0=t["sq"][:], scalar1=MAGIC - 127.0,
                    scalar2=MAGIC + 127.0, op0=ALU.max, op1=ALU.min),
                  inc=1, tag=f"v_w1q{i}")
                E("vector", lambda e, i=i: e.tensor_scalar(
                    out=t["xq"][:], in0=t["sq"][:], scalar1=-MAGIC,
                    scalar2=None, op0=ALU.add, op1=ALU.add,
                    accum_out=t["colsum1"][:, i:i+1]),
                  inc=1, tag=f"s_w1{i}")
                wtr = [f"s_w1{i}"]
                if i == 0:
                    wtr.append(f"v_gf{NPAIR-1}")
                else:
                    wtr.append(f"v_w1e{i-1}")
                for dc in range(8):
                    E("tensor", lambda e, dc=dc, wtr=wtr: e.transpose(
                        PKX[:, dc * 128:(dc + 1) * 128],
                        t["xq"][:, dc * 128:(dc + 1) * 128],
                        t["ident_bf"][:]),
                      waits=(wtr if dc == 0 else ()),
                      inc=(1 if dc == 7 else 0),
                      tag=(f"t_w1T{i}" if dc == 7 else None))
                drain(f"d_w1T{i}")
                E("vector", lambda e, i=i: e.tensor_copy(
                    out=t["AX"].ap()[:, i * 1024:(i + 1) * 1024],
                    in_=PKX),
                  waits=(f"d_w1T{i}",), inc=1,
                  tag=f"v_w1e{i}")

            E("sync", lambda e: e.dma_start(
                out=t["sc"][0:1, AMAX:AMAX+2],
                in_=d_mm1o.rearrange("(a b) -> a b", a=1)),
              waits=("cc_mm1",), dma=True, tag="y_mm1o")
            E("vector", lambda e: e.tensor_tensor(
                out=t["sc"][0:1, SX:SX+1], in0=t["sc"][0:1, AMAX:AMAX+1],
                in1=t["sc"][0:1, NEGAMIN:NEGAMIN+1], op=ALU.add),
              waits=("y_mm1o",), inc=0)
            vpad()
            E("vector", lambda e: e.tensor_scalar(
                out=t["sc"][0:1, SX:SX+1], in0=t["sc"][0:1, SX:SX+1],
                scalar1=1.0 / 255.0, scalar2=1e-8, op0=ALU.mult,
                op1=ALU.max), inc=0)
            vpad()
            E("vector", lambda e: e.reciprocal(
                out=t["sc"][0:1, INVSX:INVSX+1], in_=t["sc"][0:1, SX:SX+1]),
              inc=0)
            vpad()
            E("vector", lambda e: e.tensor_tensor(
                out=t["sc"][0:1, ZX:ZX+1], in0=t["sc"][0:1, NEGAMIN:NEGAMIN+1],
                in1=t["sc"][0:1, INVSX:INVSX+1], op=ALU.mult), inc=0)
            vpad()
            E("vector", lambda e: e.tensor_scalar(
                out=t["sc"][0:1, ZX:ZX+1], in0=t["sc"][0:1, ZX:ZX+1],
                scalar1=MAGIC, scalar2=MAGIC, op0=ALU.add, op1=ALU.subtract),
              inc=0)
            vpad()
            E("vector", lambda e: e.tensor_scalar(
                out=t["sc"][0:1, ZX:ZX+1], in0=t["sc"][0:1, ZX:ZX+1],
                scalar1=-128.0, scalar2=-128.0, op0=ALU.add, op1=ALU.max),
              inc=0)
            vpad()
            E("vector", lambda e: e.tensor_scalar(
                out=t["sc"][0:1, ZX:ZX+1], in0=t["sc"][0:1, ZX:ZX+1],
                scalar1=127.0, scalar2=MAGIC, op0=ALU.min, op1=ALU.add),
              inc=0)
            vpad()
            E("vector", lambda e: e.tensor_scalar(
                out=t["sc"][0:1, ZXM:ZXM+1], in0=t["sc"][0:1, ZX:ZX+1],
                scalar1=0.0, scalar2=None, op0=ALU.add), inc=0)
            vpad()
            E("vector", lambda e: e.tensor_scalar(
                out=t["sc"][0:1, ZX:ZX+1], in0=t["sc"][0:1, ZX:ZX+1],
                scalar1=-MAGIC, scalar2=None, op0=ALU.add), inc=0)
            vpad()
            E("vector", lambda e: e.tensor_tensor(
                out=t["sc"][0:1, SSW1:SSW1+1], in0=t["sc"][0:1, SX:SX+1],
                in1=t["sc"][0:1, SW1:SW1+1], op=ALU.mult), inc=0)
            vpad()
            E("vector", lambda e: e.tensor_tensor(
                out=t["sc"][0:1, NEGZSS1:NEGZSS1+1],
                in0=t["sc"][0:1, ZX:ZX+1], in1=t["sc"][0:1, SSW1:SSW1+1],
                op=ALU.mult), inc=0)
            vpad()
            E("vector", lambda e: e.tensor_scalar(
                out=t["sc"][0:1, NEGZSS1:NEGZSS1+1],
                in0=t["sc"][0:1, NEGZSS1:NEGZSS1+1], scalar1=-1.0,
                scalar2=None, op0=ALU.mult), inc=0)
            vpad()
            vpad()
            E("vector", lambda e: e.engine_nop(), inc=1, tag="v_sc1")
            E("tensor", lambda e: mm(e, P_b7[:, 0:8], t["ones_f32"][0:1, :],
                                     t["sc"][0:1, 0:8], True, True),
              waits=("v_sc1", "v_bw1"),
              inc=1, tag="t_bc1")
            E("vector", lambda e: e.tensor_copy(out=t["scbc"][:, 0:8],
                                                in_=P_b7[:, 0:8]),
              waits=("t_bc1",), inc=1, tag="v_bc1")
            vpad()
            E("vector", lambda e: e.scalar_tensor_tensor(
                out=t["b1eff"][:], in0=t["colsum1"][:],
                scalar=t["scbc"][:, NEGZSS1:NEGZSS1+1], in1=t["b1T"][:],
                op0=ALU.mult, op1=ALU.add),
              waits=("s_w131", "v_b1T",
                     "v_bc1"),
              inc=0)
            vpad()
            E("vector", lambda e: e.engine_nop(), inc=1, tag="v_b1eff")
            E("vector", lambda e: e.memset(t["mmx"][:, 0:1], -1e30), inc=0)
            E("vector", lambda e: e.memset(t["mmx"][:, 1:2], -1e30), inc=1,
              tag="v_mmh0")

            if phase_limit <= 5:
                return
            for ti in range(NST):
                xqf = t["xqfm0"] if ti % 2 == 0 else t["xqfm1"]
                for j in range(4):
                    x1r = t["x1r0"] if j % 2 == 0 else t["x1r1"]
                    wlx = ["v_bc1"]
                    wdma = [f"y_x1{NPAIR-1}"]
                    pj = (ti * 4 + j) - 2
                    if pj >= 0:
                        wdma.append(f"v_xq{pj}")
                    E("sync", lambda e, x1r=x1r, ti=ti, j=j, wdma=wdma:
                      e.dma_start(
                        out=x1r[:], in_=d_x1[ti * 4 + j]),
                      waits=wdma, dma=True,
                      tag=f"y_x1r{ti*4+j}")
                    if ti * 4 + j >= 1:
                        wlx = wlx + [f"s_xq{ti*4+j-1}"]
                    E("vector", lambda e, x1r=x1r, wlx=wlx: e.tensor_scalar(
                        out=t["sq"][:], in0=x1r[:],
                        scalar1=t["scbc"][:, INVSX:INVSX+1],
                        scalar2=t["scbc"][:, ZXM:ZXM+1], op0=ALU.mult,
                        op1=ALU.add),
                      waits=wlx + [f"y_x1r{ti*4+j}"], inc=0)
                    E("vector", lambda e: e.tensor_scalar(
                        out=t["sq"][:], in0=t["sq"][:],
                        scalar1=MAGIC - 128.0, scalar2=MAGIC + 127.0,
                        op0=ALU.max, op1=ALU.min), inc=1,
                      tag=f"v_xq{ti*4+j}")
                    wsq = [f"v_xq{ti*4+j}"]
                    if (ti * 4 + j) % 4 != 0:
                        wsq.append(f"t_xqT{ti*4+j-1}")
                    E("scalar", lambda e, wsq=wsq: e.activation(
                        out=t["xq"][:], in_=t["sq"][:], func=AF.Identity,
                        bias=t["negMcol"][:, 0:1], scale=1.0),
                      waits=wsq, inc=1,
                      tag=f"s_xq{ti*4+j}")
                    wtr = [f"s_xq{ti*4+j}"]
                    if ti == 0 and j == 0:
                        wtr.append("v_w1e31")
                    else:
                        wtr.append(f"v_xqe{ti*4+j-1}")
                    for dc in range(8):
                        E("tensor", lambda e, dc=dc, wtr=wtr: e.transpose(
                            PKX[:, dc * 128:(dc + 1) * 128],
                            t["xq"][:, dc * 128:(dc + 1) * 128],
                            t["ident_bf"][:]),
                          waits=(wtr if dc == 0 else ()),
                          inc=(1 if dc == 7 else 0),
                          tag=(f"t_xqT{ti*4+j}" if dc == 7 else None))
                    drain(f"d_xqT{ti*4+j}")
                    wxqe = [f"d_xqT{ti*4+j}"]
                    if ti >= 2 and j == 0:
                        wxqe.append(f"t_h{(ti-2)*NOC+NOC-1}")
                    E("vector", lambda e, xqf=xqf, j=j, wxqe=wxqe:
                      e.tensor_copy(
                        out=xqf[:, j * 1024:(j + 1) * 1024], in_=PKX),
                      waits=wxqe, inc=1,
                      tag=f"v_xqe{ti*4+j}")
                for oc in range(NOC):
                    pb = P4[:, (oc % 4) * 512:(oc % 4) * 512 + 512]
                    wlm = [f"v_xqe{ti*4+3}"]
                    if oc >= 4 or ti > 0:
                        prev = ti * NOC + oc - 4
                        if prev >= 0:
                            wlm.append(f"s_h{prev}")
                    for kk in range(8):
                        E("tensor", lambda e, pb=pb, kk=kk, oc=oc, xqf=xqf,
                          wlm=wlm: mm(
                            e, pb[:],
                            t["W1qT"][:, oc * 1024 + kk * 128:
                                      oc * 1024 + kk * 128 + 128],
                            bass.AP(t["AG2"].ap().tensor,
                                    B["xqfm0" if ti % 2 == 0 else
                                      "xqfm1"][1] + kk * 128,
                                    [[G2C, 128], [1024, 4], [1, 128]]),
                            kk == 0, kk == 7),
                          waits=(wlm if kk == 0 else ()),
                          inc=(1 if kk == 7 else 0),
                          tag=(f"t_h{ti*NOC+oc}" if kk == 7 else None))
                    hf = t["hf0"] if oc % 2 == 0 else t["hf1"]
                    E("scalar", lambda e, pb=pb, hf=hf, oc=oc: e.activation(
                        out=hf[:], in_=pb[:], func=AF.Gelu,
                        scale=t["scbc"][:, SSW1:SSW1+1],
                        bias=t["b1eff"][:, oc:oc+1]),
                      waits=(f"t_h{ti*NOC+oc}",
                             "v_b1eff"),
                      inc=1, tag=f"s_h{ti*NOC+oc}")
                    hb = t["hbf"][:, (oc % 4) * 512:(oc % 4) * 512 + 512]
                    wlh = [f"s_h{ti*NOC+oc}"]
                    if oc >= 4 or ti > 0:
                        prev = ti * NOC + oc - 4
                        if prev >= 0:
                            wlh.append(f"y_h{prev}")
                    E("vector", lambda e, hf=hf, hb=hb, oc=oc, wlh=wlh:
                      e.tensor_scalar(
                        out=hb, in0=hf[:], scalar1=0.0, scalar2=None,
                        op0=ALU.add, op1=ALU.max,
                        accum_out=t["maxc"][:, oc:oc+1]),
                      waits=wlh, inc=1, tag=f"v_h{ti*NOC+oc}")
                    mintrack = "gpsimd" if min_on_gpsimd else "vector"
                    wmg = [f"v_h{ti*NOC+oc}"]
                    if oc == 0 and ti >= 1:
                        wmg.append(f"v_hm{ti-1}")
                    E(mintrack, lambda e, hb=hb, oc=oc, wmg=wmg: e.tensor_scalar(
                        out=t["hjunk"][:], in0=hb, scalar1=-1.0,
                        scalar2=None, op0=ALU.mult, op1=ALU.max,
                        accum_out=t["nminc"][:, oc:oc+1]),
                      waits=wmg, inc=1,
                      tag=f"g_h{ti*NOC+oc}")
                    E("sync", lambda e, hb=hb, ti=ti, oc=oc: e.dma_start(
                        out=d_h[ti, oc], in_=hb),
                      waits=(f"g_h{ti*NOC+oc}",), dma=True,
                      tag=f"y_h{ti*NOC+oc}")
                E("vector", lambda e: e.tensor_scalar(
                    out=t["hjunk"][:, 0:32], in0=t["maxc"][:], scalar1=0.0,
                    scalar2=None, op0=ALU.add, op1=ALU.max,
                    accum_out=t["col3"][:]),
                  waits=(f"v_h{ti*NOC+NOC-1}",), inc=0)
                E("vector", lambda e: e.tensor_scalar(
                    out=t["hjunk"][:, 0:32], in0=t["nminc"][:], scalar1=0.0,
                    scalar2=None, op0=ALU.add, op1=ALU.max,
                    accum_out=t["col2"][:]),
                  waits=(f"g_h{ti*NOC+NOC-1}",), inc=0)
                E("vector", lambda e: e.tensor_tensor(
                    out=t["mmx"][:, 0:1], in0=t["mmx"][:, 0:1],
                    in1=t["col3"][:], op=ALU.max), inc=0)
                E("vector", lambda e: e.tensor_tensor(
                    out=t["mmx"][:, 1:2], in0=t["mmx"][:, 1:2],
                    in1=t["col2"][:], op=ALU.max), inc=1, tag=f"v_hm{ti}")

            E("vector", lambda e: e.tensor_copy(out=t["tpad"][:],
                                                in_=t["mmx"][:]),
              waits=(f"v_hm{NST-1}",), inc=1, tag="v_tp2")
            E("tensor", lambda e: e.transpose(P_s[0:2, 0:128], t["tpad"][:],
                                              t["ident_f32"][:]),
              waits=("v_tp2",), inc=1, tag="t_tp2")
            drain("d_tp2")
            E("vector", lambda e: e.tensor_copy(out=t["mm2"][:],
                                                in_=P_s[0:2, 0:128]),
              waits=("d_tp2",), inc=0)
            vpad()
            E("vector", lambda e: e.tensor_scalar(
                out=t["hjunk"][0:2, 0:128], in0=t["mm2"][:], scalar1=0.0,
                scalar2=None, op0=ALU.add, op1=ALU.max,
                accum_out=t["mm2c"][:]),
              inc=1, tag="v_mmf2")
            E("sync", lambda e: e.dma_start(
                out=d_mm2i.rearrange("(a b) -> a b", b=1), in_=t["mm2c"][:]),
              waits=("v_mmf2",), dma=True, tag="y_mm2")
            E("gpsimd", lambda e: e.collective_compute(
                "AllReduce", ALU.max, replica_groups=[list(range(8))],
                ins=[d_mm2i.opt()], outs=[d_mm2o.opt()]),
              waits=("y_mm2",), cc=True, tag="cc_mm2")

            if dbg2:
                E("sync", lambda e: e.dma_start(
                    out=d2_xqf, in_=t["xqfm0"][:]),
                  waits=("v_xqe3",), dma=True, tag="ydbg_xqf")
                E("sync", lambda e: e.dma_start(
                    out=d2_w1t, in_=t["AX"].ap()[:, 0:32768]),
                  waits=("v_w1e31",), dma=True, tag="ydbg_w1t")
                E("sync", lambda e: e.dma_start(
                    out=d2_cs1, in_=t["colsum1"][:]),
                  waits=("s_w131",), dma=True, tag="ydbg_cs1")
                E("sync", lambda e: e.dma_start(
                    out=d2_scb, in_=t["scbc"][:]),
                  waits=("v_b1eff",), dma=True, tag="ydbg_scb")
                E("sync", lambda e: e.dma_start(
                    out=d2_b1e, in_=t["b1eff"][:]),
                  dma=True, tag="ydbg_b1e")
            if phase_limit <= 6:
                return
            E("vector", lambda e: e.memset(t["wabs"][:], 0.0),
              waits=("v_mmf2",), inc=1, tag="v_wz2")
            for i in range(8):
                wt = t["w2t0"] if i % 2 == 0 else t["w2t1"]
                wlw = ["v_wz2"]
                if i >= 2:
                    wlw.append(f"v_wa2_{i-2}")
                wl2a = []
                if i == 0:
                    wl2a.append(f"v_xq{NST*4-1}")
                if i >= 2:
                    wl2a.append(f"v_wa2_{i-2}")
                E("sync", lambda e, wt=wt, i=i, wl2a=wl2a: e.dma_start(
                    out=wt[:], in_=dW2[i * 128:(i + 1) * 128, :]),
                  waits=wl2a, dma=True, tag=f"y_w2a{i}")
                if i == 0:
                    wlw.append(f"s_xq{NST*4-1}")
                E("vector", lambda e, wt=wt, wlw=wlw: e.tensor_scalar(
                    out=t["sq4"][:], in0=wt[:], scalar1=0.0, scalar2=None,
                    op0=ALU.add, op1=ALU.max, accum_out=t["col3"][:]),
                  waits=wlw + [f"y_w2a{i}"], inc=0)
                E("vector", lambda e, wt=wt: e.tensor_scalar(
                    out=t["sq4"][:], in0=wt[:], scalar1=-1.0, scalar2=None,
                    op0=ALU.mult, op1=ALU.max, accum_out=t["col2"][:]),
                  inc=0)
                E("vector", lambda e: e.tensor_tensor(
                    out=t["wabs"][:], in0=t["wabs"][:], in1=t["col3"][:],
                    op=ALU.max), inc=0)
                E("vector", lambda e: e.tensor_tensor(
                    out=t["wabs"][:], in0=t["wabs"][:], in1=t["col2"][:],
                    op=ALU.max), inc=1, tag=f"v_wa2_{i}")
            vpad()
            E("vector", lambda e: e.tensor_copy(out=t["tpad"][:, 0:1],
                                                in_=t["wabs"][:]),
              waits=("v_wa2_7",), inc=0)
            E("vector", lambda e: e.tensor_copy(out=t["tpad"][:, 1:2],
                                                in_=t["wabs"][:]),
              inc=1, tag="v_wtp2")
            E("tensor", lambda e: e.transpose(P_s[0:2, 0:128], t["tpad"][:],
                                              t["ident_f32"][:]),
              waits=("v_wtp2",), inc=1, tag="t_wtp2")
            drain("d_wtp2")
            E("vector", lambda e: e.tensor_copy(out=t["mm2"][:],
                                                in_=P_s[0:2, 0:128]),
              waits=("d_wtp2",), inc=0)
            vpad()
            E("vector", lambda e: e.tensor_scalar(
                out=t["sq4"][0:1, 0:128], in0=t["mm2"][0:1, :], scalar1=0.0,
                scalar2=None, op0=ALU.add, op1=ALU.max,
                accum_out=t["sc"][0:1, SW2:SW2+1]), inc=0)
            vpad()
            E("vector", lambda e: e.tensor_scalar(
                out=t["sc"][0:1, SW2:SW2+1], in0=t["sc"][0:1, SW2:SW2+1],
                scalar1=1.0 / 127.0, scalar2=1e-8, op0=ALU.mult,
                op1=ALU.max), inc=0)
            vpad()
            E("vector", lambda e: e.reciprocal(
                out=t["sc"][0:1, INVSW2:INVSW2+1],
                in_=t["sc"][0:1, SW2:SW2+1]), inc=0)
            vpad()
            E("vector", lambda e: e.engine_nop(), inc=1, tag="v_sw2")
            E("tensor", lambda e: mm(e, P_b7[:, 1:2], t["ones_f32"][0:1, :],
                                     t["sc"][0:1, INVSW2:INVSW2+1],
                                     True, True),
              waits=("v_sw2", "v_bc1"),
              inc=1, tag="t_bw2")
            E("vector", lambda e: e.tensor_copy(out=t["scbc"][:, 1:2],
                                                in_=P_b7[:, 1:2]),
              waits=("t_bw2",), inc=1, tag="v_bw2")
            vpad()
            for i in range(8):
                wt = t["w2t0"]
                wlw = ["v_bw2"]
                if i >= 2:
                    wlw.append(f"v_w2e{i-2}_3")
                wldb2 = ["v_wa2_7"]
                if i >= 1:
                    wldb2.append(f"v_w2m{i-1}")
                E("sync", lambda e, wt=wt, i=i: e.dma_start(
                    out=wt[:], in_=dW2[i * 128:(i + 1) * 128, :]),
                  waits=wldb2, dma=True,
                  tag=f"y_w2b{i}")
                E("vector", lambda e, wt=wt, wlw=wlw: e.tensor_scalar(
                    out=t["sq4"][:], in0=wt[:], scalar1=t["scbc"][:, 1:2],
                    scalar2=MAGIC, op0=ALU.mult, op1=ALU.add),
                  waits=wlw + [f"y_w2b{i}"], inc=1, tag=f"v_w2m{i}")
                E("vector", lambda e: e.tensor_scalar(
                    out=t["sq4"][:], in0=t["sq4"][:], scalar1=MAGIC - 127.0,
                    scalar2=MAGIC + 127.0, op0=ALU.max, op1=ALU.min),
                  inc=1, tag=f"v_w2q{i}")
                ww2 = [f"v_w2q{i}"]
                if i == 0:
                    ww2.append(f"t_h{NST*NOC-1}")
                E("vector", lambda e, i=i, ww2=ww2: e.tensor_scalar(
                    out=t["w2qbf"][:], in0=t["sq4"][:], scalar1=-MAGIC,
                    scalar2=None, op0=ALU.add, op1=ALU.add,
                    accum_out=t["colsum2"][:, i:i+1]),
                  waits=ww2, inc=1, tag=f"s_w2{i}")
                for sbch in range(4):
                    wtr = [f"s_w2{i}"]
                    if i == 0 and sbch == 0:
                        wtr.append(f"v_xqe{NST*4-1}")
                        wtr.append(f"t_h{NST*NOC-1}")
                    else:
                        prev = (i * 4 + sbch) - 1
                        wtr.append(f"v_w2e{prev//4}_{prev%4}")
                    for dc in range(8):
                        hc = sbch * 8 + dc
                        E("tensor", lambda e, hc=hc, dc=dc, wtr=wtr: e.transpose(
                            PKX[:, dc * 128:(dc + 1) * 128],
                            t["w2qbf"][:, hc * 128:(hc + 1) * 128],
                            t["ident_bf"][:]),
                          waits=(wtr if dc == 0 else ()),
                          inc=(1 if dc == 7 else 0),
                          tag=(f"t_w2T{i}_{sbch}" if dc == 7 else None))
                    drain(f"d_w2T{i}_{sbch}")
                    E("vector", lambda e, i=i, sbch=sbch: e.tensor_copy(
                        out=t["AX"].ap()[:, i * 4096 + sbch * 1024:
                                         i * 4096 + (sbch + 1) * 1024],
                        in_=PKX),
                      waits=(f"d_w2T{i}_{sbch}",), inc=1,
                      tag=f"v_w2e{i}_{sbch}")

            E("sync", lambda e: e.dma_start(
                out=t["sc"][0:1, HMAX:HMAX+2],
                in_=d_mm2o.rearrange("(a b) -> a b", a=1)),
              waits=("cc_mm2",), dma=True, tag="y_mm2o")
            E("vector", lambda e: e.tensor_tensor(
                out=t["sc"][0:1, SH:SH+1], in0=t["sc"][0:1, HMAX:HMAX+1],
                in1=t["sc"][0:1, HNEGMIN:HNEGMIN+1], op=ALU.add),
              waits=("y_mm2o",), inc=0)
            vpad()
            E("vector", lambda e: e.tensor_scalar(
                out=t["sc"][0:1, SH:SH+1], in0=t["sc"][0:1, SH:SH+1],
                scalar1=1.0 / 255.0, scalar2=1e-8, op0=ALU.mult,
                op1=ALU.max), inc=0)
            vpad()
            E("vector", lambda e: e.reciprocal(
                out=t["sc"][0:1, INVSH:INVSH+1], in_=t["sc"][0:1, SH:SH+1]),
              inc=0)
            vpad()
            E("vector", lambda e: e.tensor_tensor(
                out=t["sc"][0:1, ZH:ZH+1],
                in0=t["sc"][0:1, HNEGMIN:HNEGMIN+1],
                in1=t["sc"][0:1, INVSH:INVSH+1], op=ALU.mult), inc=0)
            vpad()
            E("vector", lambda e: e.tensor_scalar(
                out=t["sc"][0:1, ZH:ZH+1], in0=t["sc"][0:1, ZH:ZH+1],
                scalar1=MAGIC, scalar2=MAGIC, op0=ALU.add, op1=ALU.subtract),
              inc=0)
            vpad()
            E("vector", lambda e: e.tensor_scalar(
                out=t["sc"][0:1, ZH:ZH+1], in0=t["sc"][0:1, ZH:ZH+1],
                scalar1=-128.0, scalar2=-128.0, op0=ALU.add, op1=ALU.max),
              inc=0)
            vpad()
            E("vector", lambda e: e.tensor_scalar(
                out=t["sc"][0:1, ZH:ZH+1], in0=t["sc"][0:1, ZH:ZH+1],
                scalar1=127.0, scalar2=MAGIC, op0=ALU.min, op1=ALU.add),
              inc=0)
            vpad()
            E("vector", lambda e: e.tensor_scalar(
                out=t["sc"][0:1, ZHM:ZHM+1], in0=t["sc"][0:1, ZH:ZH+1],
                scalar1=0.0, scalar2=None, op0=ALU.add), inc=0)
            vpad()
            E("vector", lambda e: e.tensor_scalar(
                out=t["sc"][0:1, ZH:ZH+1], in0=t["sc"][0:1, ZH:ZH+1],
                scalar1=-MAGIC, scalar2=None, op0=ALU.add), inc=0)
            vpad()
            E("vector", lambda e: e.tensor_tensor(
                out=t["sc"][0:1, SSW2:SSW2+1], in0=t["sc"][0:1, SH:SH+1],
                in1=t["sc"][0:1, SW2:SW2+1], op=ALU.mult), inc=0)
            vpad()
            E("vector", lambda e: e.tensor_tensor(
                out=t["sc"][0:1, NEGZSS2:NEGZSS2+1],
                in0=t["sc"][0:1, ZH:ZH+1], in1=t["sc"][0:1, SSW2:SSW2+1],
                op=ALU.mult), inc=0)
            vpad()
            E("vector", lambda e: e.tensor_scalar(
                out=t["sc"][0:1, NEGZSS2:NEGZSS2+1],
                in0=t["sc"][0:1, NEGZSS2:NEGZSS2+1], scalar1=-1.0,
                scalar2=None, op0=ALU.mult), inc=0)
            vpad()
            vpad()
            E("vector", lambda e: e.engine_nop(), inc=1, tag="v_sc2")
            E("tensor", lambda e: mm(e, P_b7[:, 8:16], t["ones_f32"][0:1, :],
                                     t["sc"][0:1, 12:20], True, True),
              waits=("v_sc2", "v_bw2"),
              inc=1, tag="t_bc2")
            E("vector", lambda e: e.tensor_copy(out=t["scbc"][:, 8:16],
                                                in_=P_b7[:, 8:16]),
              waits=("t_bc2",), inc=1, tag="v_bc2")
            vpad()
            SB_INVSH, SB_ZHM, SB_SSW2, SB_NEGZSS2 = 11, 13, 14, 15
            E("vector", lambda e: e.scalar_tensor_tensor(
                out=t["b2eff"][:], in0=t["colsum2"][:],
                scalar=t["scbc"][:, SB_NEGZSS2:SB_NEGZSS2+1],
                in1=t["b2T"][:], op0=ALU.mult, op1=ALU.add),
              waits=("s_w27", "v_b2T",
                     "v_bc2"),
              inc=0)
            vpad()
            E("vector", lambda e: e.engine_nop(), inc=1, tag="v_b2eff")
            for t8 in range(8):
                E("sync", lambda e, t8=t8: e.dma_start(
                    out=t["b2eff_row"][0:1, t8 * 128:(t8 + 1) * 128],
                    in_=t["b2eff"][:, t8:t8+1]),
                  waits=("v_b2eff",), dma=True,
                  tag=f"y_b2r{t8}")
            E("tensor", lambda e: mm(e, P4[:, 0:512], t["ones_f32"][0:1, :],
                                     t["b2eff_row"][0:1, 0:512], True, True),
              waits=tuple(f"y_b2r{i}" for i in range(8))
                    + (f"s_h{NST*NOC-1}",),
              inc=0)
            E("tensor", lambda e: mm(e, P4[:, 512:1024],
                                     t["ones_f32"][0:1, :],
                                     t["b2eff_row"][0:1, 512:1024],
                                     True, True),
              inc=1, tag="t_b2bc")
            E("vector", lambda e: e.tensor_copy(out=t["b2eff_bc"][:],
                                                in_=P4[:, 0:1024]),
              waits=("t_b2bc",), inc=1, tag="v_b2bc")

            if phase_limit <= 7:
                return
            for ti in range(NST):
                for g8 in range(4):
                    hs = t["hstage0"] if g8 % 2 == 0 else t["hstage1"]
                    wlh = []
                    if ti == 0 and g8 == 0:
                        wlh.append(f"t_o{NPAIR-1}")
                    if ti * 4 + g8 >= 2:
                        pg = ti * 4 + g8 - 2
                        wlh.append(f"s_hq{pg}")
                    E("sync", lambda e, hs=hs, ti=ti, g8=g8: e.dma_start(
                        out=hs[:].rearrange("p (a b) -> p a b", a=8),
                        in_=d_h[ti, g8 * 8:(g8 + 1) * 8]
                            .rearrange("a p b -> p a b")),
                      waits=wlh + [f"y_h{ti*NOC+NOC-1}"],
                      dma=True, tag=f"y_hr{ti*4+g8}")
                    wts1 = [f"y_hr{ti*4+g8}", "v_b2bc"]
                    if ti * 4 + g8 >= 1:
                        wts1.append(f"s_hq{ti*4+g8-1}")
                    if g8 == 0 and ti >= 1:
                        wts1 += [f"s_oq{ti*4-1}", f"y_out{ti*4-1}",
                                 f"y_out{ti*4-2}"]
                    E("vector", lambda e, hs=hs, wts1=wts1: e.tensor_scalar(
                        out=t["sq4"][:], in0=hs[:],
                        scalar1=t["scbc"][:, SB_INVSH:SB_INVSH+1],
                        scalar2=t["scbc"][:, SB_ZHM:SB_ZHM+1],
                        op0=ALU.mult, op1=ALU.add),
                      waits=wts1,
                      inc=0)
                    E("vector", lambda e: e.tensor_scalar(
                        out=t["sq4"][:], in0=t["sq4"][:],
                        scalar1=MAGIC - 128.0, scalar2=MAGIC + 127.0,
                        op0=ALU.max, op1=ALU.min), inc=1,
                      tag=f"v_hq{ti*4+g8}")
                    whq = [f"v_hq{ti*4+g8}"]
                    if ti == 0 and g8 == 0:
                        whq.append("t_w2T7_3")
                    if ti >= 1 and g8 == 0:
                        whq.append(f"t_w2m{ti*4-1}")
                    E("scalar", lambda e, g8=g8, whq=whq: e.activation(
                        out=t["hq"][:, g8 * 4096:(g8 + 1) * 4096],
                        in_=t["sq4"][:], func=AF.Identity,
                        bias=t["negMcol"][:, 0:1], scale=1.0),
                      waits=whq, inc=1,
                      tag=f"s_hq{ti*4+g8}")
                for tchunk in range(4):
                    pb = P4[:, (tchunk % 2) * 1024:(tchunk % 2) * 1024 + 1024]
                    wlm = [f"s_hq{ti*4+3}"]
                    if ti == 0 and tchunk == 0:
                        wlm.append("v_w2e7_3")
                    if ti * 4 + tchunk >= 2:
                        prev = ti * 4 + tchunk - 2
                        wlm.append(f"v_o{prev}")
                    for no in range(2):
                        for kk in range(32):
                            E("tensor", lambda e, pb=pb, kk=kk, no=no,
                              tchunk=tchunk, wlm=wlm: mm(
                                e, pb[:, no * 512: no * 512 + 512],
                                t["hq"][:, kk * 512 + tchunk * 128:
                                        kk * 512 + tchunk * 128 + 128],
                                bass.AP(t["AX"].ap().tensor,
                                        no * 4 * 4096 + (kk // 8) * 1024
                                        + (kk % 8) * 128,
                                        [[XCOLS, 128], [4096, 4], [1, 128]]),
                                kk == 0, kk == 31),
                              waits=(wlm if (no == 0 and kk == 0) else ()),
                              inc=(1 if (no == 1 and kk == 31) else 0),
                              tag=(f"t_w2m{ti*4+tchunk}"
                                   if (no == 1 and kk == 31) else None))
                    x1r = t["x1r0"] if tchunk % 2 == 0 else t["x1r1"]
                    wlx = []
                    if ti == 0 and tchunk == 0:
                        wlx.append("v_w2q7")
                    if ti * 4 + tchunk >= 2:
                        wlx.append(f"v_o{ti*4+tchunk-2}")
                    E("sync", lambda e, x1r=x1r, ti=ti, tchunk=tchunk, wlx=wlx:
                      e.dma_start(out=x1r[:], in_=d_x1[ti * 4 + tchunk]),
                      waits=wlx, dma=True, tag=f"y_x1s{ti*4+tchunk}")
                    E("vector", lambda e, pb=pb: e.scalar_tensor_tensor(
                        out=t["r1"][:], in0=pb[:],
                        scalar=t["scbc"][:, SB_SSW2:SB_SSW2+1],
                        in1=t["b2eff_bc"][:], op0=ALU.mult, op1=ALU.add),
                      waits=(f"t_w2m{ti*4+tchunk}",), inc=0)
                    E("vector", lambda e, x1r=x1r: e.tensor_tensor(
                        out=t["r1"][:], in0=t["r1"][:], in1=x1r[:],
                        op=ALU.add),
                      waits=(f"y_x1s{ti*4+tchunk}",), inc=1,
                      tag=f"v_r2{ti*4+tchunk}")
                    ln_stats("o", ti * 4 + tchunk, t["r1"],
                             w=[f"v_r2{ti*4+tchunk}"], junk=t["silb"])
                    wlf = [f"v_on{ti*4+tchunk}"]
                    if ti * 4 + tchunk >= 1:
                        wlf.append(f"y_out{ti*4+tchunk-1}")
                    E("vector", lambda e: e.tensor_scalar(
                        out=t["x1t"][:], in0=t["r1"][:],
                        scalar1=t["col1"][:, 0:1], scalar2=t["col2"][:, 0:1],
                        op0=ALU.mult, op1=ALU.add),
                      waits=wlf, inc=1, tag=f"s_out{ti*4+tchunk}")
                    E("vector", lambda e: e.engine_nop(),
                      waits=(f"s_out{ti*4+tchunk}",), inc=1,
                      tag=f"v_o{ti*4+tchunk}")
                    E("sync", lambda e, ti=ti, tchunk=tchunk: e.dma_start(
                        out=dout[(ti * 4 + tchunk) * 128:
                                 (ti * 4 + tchunk + 1) * 128, :],
                        in_=t["x1t"][:]),
                      waits=(f"s_out{ti*4+tchunk}",),
                      dma=True, tag=f"y_out{ti*4+tchunk}")
            E("sync", lambda e: e.nop(),
              waits=(f"y_out{NTI-1}",), flush=True)

        scr = {"d_k": d_k, "d_vt": d_vt, "d_at": d_at, "d_sloc": d_sloc,
               "d_gn": d_gn, "d_x1": d_x1, "d_h": d_h, "d_agout": d_agout}
        dbg_outs = {}
        for name, _wt in dumps:
            s = scr[name]
            dbg_outs[name] = nc.dram_tensor(
                "dbg_" + name, list(s.tensor.shape), s.tensor.dtype,
                kind="ExternalOutput").ap()

        def walk_all(em):
            walk(em)
            E = em.op
            last = None
            for i, (name, wt) in enumerate(dumps):
                last = f"ydbg{i}"
                E("sync", lambda e, name=name: e.dma_start(
                    out=dbg_outs[name], in_=scr[name]),
                  waits=wt, dma=True, tag=last)
            if last is not None:
                E("sync", lambda e: e.nop(), waits=(last,))

        em = Em("plan", sems, cp, cptr)
        walk_all(em)

        with nc.Block() as block:
            def runner(name):
                def _f(engine):
                    em2 = Em(name, sems, cp, cptr)
                    em2.h = engine
                    walk_all(em2)
                return _f
            block.tensor(runner("tensor"))
            block.vector(runner("vector"))
            block.scalar(runner("scalar"))
            block.gpsimd(runner("gpsimd"))
            block.sync(runner("sync"))

    return nc


_NC_CACHE = {}


def _get_nc(T):
    if T not in _NC_CACHE:
        _NC_CACHE[T] = build_nc(T)
    return _NC_CACHE[T]


_SCRAMBLE = np.concatenate([np.arange(64) + (2 * hq + hp) * 64
                            for hp in range(2) for hq in range(8)])


def make_inputs(x_full, W, T):
    B, S, _ = x_full.shape
    Tc = B * S // 8
    assert Tc == T
    per_batch = 8 // B
    W = dict(W)
    W["gam"] = np.ascontiguousarray(W["gam"][:, _SCRAMBLE])
    W["bet"] = np.ascontiguousarray(W["bet"][:, _SCRAMBLE])
    W["Wg"] = np.ascontiguousarray(W["Wg"][:, _SCRAMBLE])
    W["Wout"] = np.ascontiguousarray(W["Wout"][_SCRAMBLE, :])
    maps = []
    for c in range(8):
        b = c // per_batch
        t0 = (c % per_batch) * T
        m = np.zeros((128, 8), np.float32)
        for r in range(8):
            if r // per_batch == b and r < c:
                m[:, r] = 1.0
        mc = 1.0 - m
        im = {
            "x": np.ascontiguousarray(x_full[b, t0:t0 + T]),
            "m64": m, "m64c": mc,
        }
        im.update(W)
        maps.append(im)
    return maps


def kernel(L0_attn_x, L0_attn_Wq, L0_attn_Wk, L0_attn_Wv, L0_attn_Wf,
           L0_attn_bf, L0_attn_Wg, L0_attn_Wout, L0_attn_gamma, L0_attn_beta,
           L0_ffn_W1, L0_ffn_b1, L0_ffn_W2, L0_ffn_b2):
    x = np.asarray(L0_attn_x, np.float32)
    B, S, _ = x.shape
    T = B * S // 8
    W = {
        "Wq": np.asarray(L0_attn_Wq, np.float32),
        "Wk": np.asarray(L0_attn_Wk, np.float32),
        "Wv": np.asarray(L0_attn_Wv, np.float32),
        "Wf": np.asarray(L0_attn_Wf, np.float32),
        "Wg": np.asarray(L0_attn_Wg, np.float32),
        "Wout": np.asarray(L0_attn_Wout, np.float32),
        "bfv": np.asarray(L0_attn_bf, np.float32).reshape(1, D),
        "gam": np.asarray(L0_attn_gamma, np.float32).reshape(1, D),
        "bet": np.asarray(L0_attn_beta, np.float32).reshape(1, D),
        "W1": np.asarray(L0_ffn_W1, np.float32),
        "b1": np.asarray(L0_ffn_b1, np.float32).reshape(32, 128),
        "W2": np.asarray(L0_ffn_W2, np.float32),
        "b2": np.asarray(L0_ffn_b2, np.float32).reshape(8, 128),
    }
    nc = _get_nc(T)
    maps = make_inputs(x, W, T)
    res = run_bass_kernel_spmd(nc, maps, core_ids=list(range(8)))
    out = np.zeros((B, S, D), np.float32)
    per_batch = 8 // B
    for c in range(8):
        b = c // per_batch
        t0 = (c % per_batch) * T
        out[b, t0:t0 + T] = res.results[c]["out"]
    return out
